# version 11
# baseline (speedup 1.0000x reference)
import os
import math
import numpy as np
import ml_dtypes
import concourse.bass as bass
import concourse.mybir as mybir
from concourse.bass_utils import run_bass_kernel_spmd

F32 = mybir.dt.float32
F16 = mybir.dt.float16
BF16 = mybir.dt.bfloat16
AX = mybir.AxisListType.X
OP = mybir.AluOpType
AF = mybir.ActivationFunctionType

DEBUG = os.environ.get("MK_DEBUG", "") == "1"
TRUNC = os.environ.get("MK_TRUNC", "") == "1"

S = 2048
D = 1024
NH, HD = 16, 64
WIN, STR = 512, 256
NW = 7
G = 64
IFF = 2816
NSEQ = 2
NT = 16
RND = 1536.0
LNEPS = 1e-5
ENGS = ("sync", "tensor", "vector", "scalar", "gpsimd")


def _host_consts():
    cur = np.eye(S)
    res = np.zeros((S, S))
    s = 1
    while s < S:
        cur = cur + 0.5 * (np.roll(cur, s, axis=0) + np.roll(cur, -s, axis=0))
        res = res + cur
        s *= 2
    C = (res / (math.log2(S) + 1.0)).astype(np.float32)
    CT = C.T
    CTL = np.ascontiguousarray(CT.reshape(16, 128, 4, 512).transpose(2, 1, 0, 3))
    tri = np.linspace(0.5, 1.5, WIN)
    den = np.zeros(S)
    for w in range(NW):
        den[w * STR : w * STR + WIN] += tri
    denr = 1.0 / (den + 1e-6)
    twden = np.zeros((128, NW * 4), dtype=np.float32)
    for w in range(NW):
        for j in range(4):
            q = w * STR + j * 128 + np.arange(128)
            twden[:, w * 4 + j] = (tri[j * 128 : (j + 1) * 128] * denr[q]).astype(np.float32)
    ident = np.eye(128, dtype=np.float32)
    ones1 = np.ones((1, 128), dtype=np.float32)
    return CTL, twden, ident, ones1


class Sem:
    def __init__(self, nc, name):
        self.h = nc.alloc_semaphore(name)
        self.n = 0
        self.name = name


class Alloc:
    def __init__(self, tensor, total):
        self.t = tensor
        self.total = total
        self.pos = 0
        self.peak = 0

    def take(self, n, align=16):
        self.pos = (self.pos + align - 1) // align * align
        assert self.pos + n <= self.total, f"arena overflow: {self.pos}+{n} > {self.total}"
        ap = self.t[:, self.pos : self.pos + n]
        self.pos += n
        self.peak = max(self.peak, self.pos)
        return ap

    def slice(self, start, n):
        return self.t[:, start : start + n]


class MK:
    def __init__(self):
        nc = bass.Bass("TRN2", target_bir_lowering=False, debug=True)
        self.nc = nc
        spill_kind = "ExternalOutput" if DEBUG else "Internal"
        self.d_x = nc.dram_tensor("x_in", [NSEQ, S, D], F32, kind="ExternalInput").ap()
        self.d_gm = nc.dram_tensor("gm_mem", [G, D], F32, kind="ExternalInput").ap()
        self.d_cwT = nc.dram_tensor("cwT", [D, 4 * D], F32, kind="ExternalInput").ap()
        self.d_convb = nc.dram_tensor("convb_pp", [128, 8], F32, kind="ExternalInput").ap()
        self.d_mixw = nc.dram_tensor("mix_w", [2 * D, D], F32, kind="ExternalInput").ap()
        self.d_mixb = nc.dram_tensor("mixb_pp", [128, 8], F32, kind="ExternalInput").ap()
        self.d_outw = nc.dram_tensor("out_w", [D, D], F32, kind="ExternalInput").ap()
        self.d_outb_bc = nc.dram_tensor("outb_bc", [128, D], F32, kind="ExternalInput").ap()
        self.d_clip = nc.dram_tensor("clips", [1, 2], F32, kind="ExternalInput").ap()
        self.d_wg = nc.dram_tensor("w_gate", [IFF, D], F32, kind="ExternalInput").ap()
        self.d_wu = nc.dram_tensor("w_up", [IFF, D], F32, kind="ExternalInput").ap()
        self.d_wd = nc.dram_tensor("w_down", [D, IFF], F32, kind="ExternalInput").ap()
        self.d_CT = nc.dram_tensor("CT", [4, 128, 16, 512], F32, kind="ExternalInput").ap()
        self.d_twden = nc.dram_tensor("twden", [128, NW * 4], F32, kind="ExternalInput").ap()
        self.d_ident = nc.dram_tensor("ident", [128, 128], F32, kind="ExternalInput").ap()
        self.d_ones1 = nc.dram_tensor("ones1", [1, 128], F32, kind="ExternalInput").ap()
        self.d_out = nc.dram_tensor("out", [NSEQ, S, D], F32, kind="ExternalOutput").ap()
        self.d_loc_h = nc.dram_tensor("sp_loc_h", [NSEQ, S, D], BF16, kind=spill_kind).ap()
        self.d_loc_l = nc.dram_tensor("sp_loc_l", [NSEQ, S, D], BF16, kind=spill_kind).ap()
        self.d_glo_h = nc.dram_tensor("sp_glo_h", [NSEQ, S, D], BF16, kind=spill_kind).ap()
        self.d_glo_l = nc.dram_tensor("sp_glo_l", [NSEQ, S, D], BF16, kind=spill_kind).ap()
        self.d_bc = nc.dram_tensor("sp_bc", [NSEQ, 4, 8, 128, 512], F32, kind=spill_kind).ap()
        self.d_x1 = nc.dram_tensor("sp_x1", [NSEQ, NT, 128, D], F32, kind=spill_kind).ap()
        self.d_h = nc.dram_tensor("sp_h", [NSEQ, 22, 128, S], F32, kind=spill_kind).ap()
        self.d_dbg1 = nc.dram_tensor("dbg_x1q", [128, 16384], BF16, kind=spill_kind).ap()
        self.d_dbg2 = nc.dram_tensor("dbg_g", [128, 512], F32, kind=spill_kind).ap()
        self.d_cc_in = nc.dram_tensor("cc_in", [1, 1], F32)
        self.d_cc_out = nc.dram_tensor("cc_out", [1, 1], F32)
        self.d_swdscr = nc.dram_tensor("swdscr", [8, 128], F32)
        AFW = 52224
        self.arena = nc.alloc_sbuf_tensor("arena", [128, AFW], F32).ap()
        self.al = Alloc(self.arena, AFW)
        self.ident = nc.alloc_sbuf_tensor("identsb", [128, 128], F32).ap()
        self.ones1 = nc.alloc_sbuf_tensor("ones1sb", [1, 128], F32).ap()
        al = self.al
        self.twden = al.take(32)
        self.bias_bc = al.take(D)
        self.swd_bc = al.take(D)
        self.mixb = al.take(8)
        self.convb = al.take(8)
        self.p_sc = al.take(16)
        self.gmax = al.take(4)
        self.zero1 = al.take(4)
        self.p_rnd = al.take(4)
        self.swg = al.take(24)
        self.swu = al.take(24)
        self.swd = al.take(8)
        self.swgx = al.take(24)
        self.swux = al.take(24)
        self.sc1 = al.take(16)
        self.scr128 = al.take(128)
        self.base_pos = al.pos
        self.pb = [nc.alloc_psum_tensor(f"pb{i}", [128, 512], F32).ap() for i in range(8)]
        self.sX = Sem(nc, "sX")
        self.sW = Sem(nc, "sW")
        self.sCT = [Sem(nc, "sCT0"), Sem(nc, "sCT1")]
        self.sST = [Sem(nc, "sST0"), Sem(nc, "sST1")]
        self.sSB = Sem(nc, "sSB")
        self.sX1 = Sem(nc, "sX1")
        self.sH = Sem(nc, "sH")
        self.sH1 = Sem(nc, "sH1")
        self.sOUT = [Sem(nc, "sOUT0"), Sem(nc, "sOUT1")]
        self.sCCD = Sem(nc, "sCCD")
        self.sBCF = Sem(nc, "sBCF")
        self.sXR = [Sem(nc, "sXR0"), Sem(nc, "sXR1")]
        self.sTP = Sem(nc, "sTP")
        self.sSP = Sem(nc, "sSP")
        self.sT = Sem(nc, "sT")
        self.sV = Sem(nc, "sV")
        self.sS = Sem(nc, "sS")
        self.sG = Sem(nc, "sG")
        self.prog = {k: [] for k in ENGS}
        self.nins = 0
        self.marks = []
        self.cur_label = "init"
        self.itrace = {k: [] for k in ENGS}

    def mark(self, label):
        self.cur_label = label
        self.marks.append((self.nins, label, {k: len(v) for k, v in self.prog.items()}))

    def _raw(self, eng, fn, inc=None, k=1):
        self.itrace[eng].append(self.cur_label)
        self.nins += 1
        if inc is not None:
            inc.n += k
            self.prog[eng].append(lambda e, f=fn, s=inc.h, kk=k: f(e).then_inc(s, kk))
        else:
            self.prog[eng].append(lambda e, f=fn: f(e))

    def wait(self, eng, sem, val=None):
        v = sem.n if val is None else val
        if v is None or v <= 0:
            return
        self.itrace[eng].append("w:" + self.cur_label)
        self.nins += 1
        self.prog[eng].append(lambda e, s=sem.h, vv=v: e.wait_ge(s, vv))

    def dma(self, eng, sem, out, in_, slow=False):
        self._raw(eng, lambda e, o=out, i=in_, sl=slow:
                  e.dma_start(out=o, in_=i, allow_slow_non_contiguous=sl), inc=sem, k=16)

    def dmatp(self, out, in_, eng="scalar", sem=None):
        self._raw(eng, lambda e, o=out, i=in_: e.dma_start_transpose(o, i),
                  inc=sem if sem is not None else self.sTP, k=16)

    def barrier(self, engs=ENGS):
        sems = [self.sX, self.sW, self.sTP, self.sSP, self.sT, self.sV, self.sS, self.sG,
                self.sCT[0], self.sCT[1], self.sST[0], self.sST[1], self.sSB, self.sX1,
                self.sH, self.sH1, self.sOUT[0], self.sOUT[1], self.sCCD, self.sBCF,
                self.sXR[0], self.sXR[1]]
        own = {"sync": (self.sX, self.sW, self.sCT[0], self.sCT[1], self.sBCF,
                        self.sXR[0], self.sXR[1]), "tensor": (self.sT,),
               "vector": (self.sV,), "scalar": (self.sS, self.sTP),
               "gpsimd": (self.sG, self.sSP, self.sST[0], self.sST[1], self.sSB, self.sX1,
                          self.sH, self.sH1, self.sOUT[0], self.sOUT[1], self.sCCD)}
        for e in engs:
            for s in sems:
                if s not in own[e] and s.n > 0:
                    self.wait(e, s)

    def _v(self, fn):
        self.wait("vector", self.sV)
        self._raw("vector", fn, inc=self.sV)

    def _s(self, fn):
        self.wait("scalar", self.sS)
        self._raw("scalar", fn, inc=self.sS)

    def mm(self, out, lhsT, rhs, start, stop, inc=False):
        self._raw("tensor", lambda e, o=out, a=lhsT, b=rhs, st=start, sp=stop:
                  e.matmul(o, a, b, start=st, stop=sp), inc=self.sT if inc else None)

    def tp(self, out, in_):
        self._raw("tensor", lambda e, o=out, i=in_, id_=self.ident: e.transpose(o, i, id_), inc=self.sT)

    def vcopy(self, out, in_):
        self._v(lambda e, o=out, i=in_: e.tensor_copy(o, i))

    def vtt(self, out, a, b, op):
        self._v(lambda e, o=out, x=a, y=b, p=op: e.tensor_tensor(o, x, y, p))

    def vts(self, out, in_, s1, s2, op1, op2=OP.bypass):
        self._v(lambda e, o=out, i=in_, a=s1, b=s2, p=op1, q=op2: e.tensor_scalar(o, i, a, b, p, q))

    def vstt(self, out, in0, scal, in1, op0, op1):
        self._v(lambda e, o=out, i=in0, s=scal, j=in1, p=op0, q=op1:
                e.scalar_tensor_tensor(o, i, s, j, p, q))

    def vred(self, out, in_, op, absv=False):
        self._v(lambda e, o=out, i=in_, p=op, a=absv:
                e.tensor_reduce(o, i, AX, p, apply_absolute_value=a))

    def vrecip(self, out, in_):
        self._v(lambda e, o=out, i=in_: e.reciprocal(o, i))

    def vmemset(self, ap, val):
        self._v(lambda e, o=ap, v=val: e.memset(o, v))

    def act(self, out, in_, func, bias=None, scale=1.0, accum=None):
        b = bias if bias is not None else self.zero1[:, 0:1]
        self._s(lambda e, o=out, i=in_, f=func, bb=b, sc=scale, ac=accum:
                e.activation(o, i, f, bias=bb, scale=sc, accum_out=ac))

    def ssqrt(self, out, in_):
        self._s(lambda e, o=out, i=in_: e.sqrt(o, i))

    def build(self):
        self.setup()
        for s in range(NSEQ):
            self.phase1(s)
        self.p15_prologue()
        for s in range(NSEQ):
            self.phase15(s)
        self.barrier()
        self.al.pos = self.base_pos
        self.wdqT = self.al.take(22 * D // 2)
        self.wd_pos = self.al.pos
        self.wgqT = self.al.take(IFF * 8 // 2)
        self.wupqT = self.al.take(IFF * 8 // 2)
        self.ffn_pos = self.al.pos
        self.ffn_weight_prep(self.d_wg, self.wgqT, self.swg)
        self.ffn_weight_prep(self.d_wu, self.wupqT, self.swu)
        self.wd_weight_prep()
        self.collective(0)
        if TRUNC:
            self.barrier()
            self.dma("gpsimd", self.sSP, self.d_dbg1, self.wgqT.bitcast(BF16)[:, 0:16384])
            self.finish()
            return self._materialize()
        for s in range(NSEQ):
            self.phase2(s)
        self.barrier()
        self.collective(1)
        for s in range(NSEQ):
            self.phase3(s)
        self.finish()
        return self._materialize()

    def _materialize(self):
        nc = self.nc
        with nc.Block() as block:
            @block.sync
            def _(e):
                for f in self.prog["sync"]:
                    f(e)
            @block.tensor
            def _(e):
                for f in self.prog["tensor"]:
                    f(e)
            @block.vector
            def _(e):
                for f in self.prog["vector"]:
                    f(e)
            @block.scalar
            def _(e):
                for f in self.prog["scalar"]:
                    f(e)
            @block.gpsimd
            def _(e):
                for f in self.prog["gpsimd"]:
                    f(e)
        return nc

    def setup(self):
        self.dma("sync", self.sX, self.ident[:, :], self.d_ident)
        self.dma("sync", self.sX, self.ones1[:, :], self.d_ones1)
        self.dma("sync", self.sX, self.twden[:, : NW * 4], self.d_twden)
        self.dma("sync", self.sX, self.bias_bc, self.d_outb_bc)
        self.dma("sync", self.sX, self.mixb, self.d_mixb)
        self.dma("sync", self.sX, self.convb, self.d_convb)
        self.dma("sync", self.sX, self.sc1[0:1, 0:2], self.d_clip)
        self.vmemset(self.zero1, 0.0)
        self.vmemset(self.gmax, 0.0)
        self.vmemset(self.p_rnd, RND)

    def phase1(self, s):
        self.mark(f'phase1({s})')
        al = self.al
        al.pos = self.base_pos
        x_tm = al.take(NT * D)
        p2start = al.pos
        P2 = al.take(4096)
        numstart = al.pos
        num = al.take(4 * D)
        xfmstart = al.pos
        xfm_h = al.take(8 * S // 2)
        xfm_l = al.take(8 * S // 2)
        stg_h = al.take(D)
        stg_l = al.take(D)
        gt_tm = al.take(D)
        gt_h = al.take(D // 2)
        gt_l = al.take(D // 2)
        gtones = al.take(NH * 65 + 7)
        srec = al.take(32)
        xfm_hb = xfm_h.bitcast(BF16)
        xfm_lb = xfm_l.bitcast(BF16)
        stg_hb = stg_h.bitcast(BF16)
        stg_lb = stg_l.bitcast(BF16)
        gt_hb = gt_h.bitcast(BF16)
        gt_lb = gt_l.bitcast(BF16)
        pb = self.pb

        if s > 0:
            self.barrier()
        self.dma("sync", self.sX,
                 x_tm.rearrange("p (c f) -> p c f", f=D),
                 self.d_x[s].rearrange("(c p) f -> p c f", p=128))
        ldx = self.sX.n

        for tc in range(NT):
            b = tc % 2
            xs = x_tm[:, tc * D : (tc + 1) * D]
            sh = stg_hb[:, b * D : (b + 1) * D]
            sl = stg_lb[:, b * D : (b + 1) * D]
            self.wait("vector", self.sX, ldx)
            if tc >= 2:
                self.wait("vector", self.sTP)
            self.vcopy(sh, xs)
            self.vtt(sl, xs, sh, OP.subtract)
            self.wait("scalar", self.sV)
            for (src, dstf) in ((sh, xfm_hb), (sl, xfm_lb)):
                self.dmatp(
                    dstf.rearrange("f (c t) -> f c t", t=S)[:, :, tc * 128 : (tc + 1) * 128],
                    src.rearrange("t (c f) -> t c f", f=128))
        tp_x = self.sTP.n

        self.mark('conv')
        x256 = al.take(2048)
        comp = al.take(512)
        cwb = al.slice(p2start, 8192)
        self.wait("tensor", self.sX, ldx)
        for i in range(16):
            tc, fch = i % 2, i // 2
            bank = pb[6 + i % 2]
            if i >= 2:
                self.wait("tensor", self.sV)
            self.tp(bank[:, 0:128], x_tm[:, tc * D + fch * 128 : tc * D + fch * 128 + 128])
            self.wait("vector", self.sT)
            self.vcopy(x256[:, fch * 256 + tc * 128 : fch * 256 + tc * 128 + 128], bank[:, 0:128])
        x256r = x256.rearrange("p (c t k) -> p c k t", k=4, t=64)
        self.wait("tensor", self.sV)
        nmm = 0
        cw_mm = {}
        for ich in range(8):
            cb = ich % 2
            if ich >= 2:
                self.wait("sync", self.sT, cw_mm[cb])
            self.dma("sync", self.sCT[cb], cwb[:, cb * 4096 : (cb + 1) * 4096],
                     self.d_cwT[ich * 128 : (ich + 1) * 128, :])
            self.wait("tensor", self.sCT[cb])
            for k in range(4):
                for och in range(8):
                    self.mm(pb[0][:, och * 64 : och * 64 + 64],
                            cwb[:, cb * 4096 + k * D + och * 128 : cb * 4096 + k * D + och * 128 + 128],
                            x256r[:, ich, k, :],
                            start=(nmm == 0), stop=(nmm == 255), inc=True)
                    nmm += 1
            cw_mm[cb] = self.sT.n
        self.wait("vector", self.sT)
        for och in range(8):
            self.vts(comp[:, och * 64 : och * 64 + 64], pb[0][:, och * 64 : och * 64 + 64],
                     self.convb[:, och : och + 1], None, OP.add)
        self.dma("sync", self.sX, gt_tm[64:128, :], self.d_gm)
        gm_n = self.sX.n
        self.wait("tensor", self.sV)
        for och in range(8):
            bank = pb[6 + och % 2]
            if och >= 2:
                self.wait("tensor", self.sV)
            self.tp(bank[0:64, 0:128], comp[:, och * 64 : och * 64 + 64])
            self.wait("vector", self.sT)
            self.vcopy(gt_tm[0:64, och * 128 : och * 128 + 128], bank[0:64, 0:128])
        self.wait("vector", self.sX, gm_n)
        self.wait("vector", self.sTP)
        self.vcopy(stg_hb[:, 0:D], gt_tm)
        self.vtt(stg_lb[:, 0:D], gt_tm, stg_hb[:, 0:D], OP.subtract)
        self.wait("scalar", self.sV)
        self.dmatp(gt_hb.rearrange("f (c k) -> f c k", k=128),
                   stg_hb[:, 0:D].rearrange("k (c f) -> k c f", f=128))
        self.dmatp(gt_lb.rearrange("f (c k) -> f c k", k=128),
                   stg_lb[:, 0:D].rearrange("k (c f) -> k c f", f=128))
        self.vmemset(gtones[:, : NH * 65], 1.0)
        for h in range(NH):
            self.vcopy(gtones[:, h * 65 : h * 65 + 64], gt_tm[:, h * 64 : h * 64 + 64])
        gt_v = self.sV.n
        gt_tp = self.sTP.n

        self.mark('cga')
        glob_ev = num
        self.wait("tensor", self.sTP, tp_x)
        self.wait("tensor", self.sTP, gt_tp)
        self.wait("tensor", self.sV, gt_v)
        guard_l = {}
        guard_p2 = {}
        guard_pv = {}
        pend = []
        for tau in range(4):
            for hp in range(8):
                lmm = {}
                for par in (0, 1):
                    if guard_l.get(par):
                        self.wait("tensor", self.sS, guard_l[par])
                for pidx in range(3):
                    for par in (0, 1):
                        h = 2 * hp + par
                        pl = pb[par]
                        bp = par * 64
                        fc = hp
                        glh = gt_hb[bp : bp + 64, fc * 128 : fc * 128 + 128]
                        gll = gt_lb[bp : bp + 64, fc * 128 : fc * 128 + 128]
                        xrh = xfm_hb[bp : bp + 64, fc * S + tau * 512 : fc * S + tau * 512 + 512]
                        xrl = xfm_lb[bp : bp + 64, fc * S + tau * 512 : fc * S + tau * 512 + 512]
                        ops = ((glh, xrh), (glh, xrl), (gll, xrh))[pidx]
                        self.mm(pl[:, :], ops[0], ops[1], start=(pidx == 0), stop=(pidx == 2),
                                inc=(pidx == 2))
                        lmm[par] = self.sT.n
                for p_ in pend:
                    self._cga_pv(*p_)
                pend = []
                for par in (0, 1):
                    h = 2 * hp + par
                    slot = P2[:, par * 512 : par * 512 + 512]
                    if guard_p2.get(par):
                        self.wait("scalar", self.sT, guard_p2[par])
                    self.wait("scalar", self.sT, lmm[par])
                    self.act(slot, pb[par][:, :], AF.Exp, scale=0.125)
                    guard_l[par] = self.sS.n
                    pend.append((tau, h, par, slot, self.sS.n, guard_pv, guard_p2, glob_ev, gtones))
            for p_ in pend:
                self._cga_pv(*p_)
            pend = []
            for qc in range(4):
                b = qc % 2
                gl = glob_ev[:, qc * D : (qc + 1) * D]
                self.wait("vector", self.sST[b])
                self.wait("vector", self.sTP)
                self.vcopy(stg_hb[:, b * D : (b + 1) * D], gl)
                self.vtt(stg_lb[:, b * D : (b + 1) * D], gl, stg_hb[:, b * D : (b + 1) * D], OP.subtract)
                self.wait("gpsimd", self.sV)
                qg = tau * 4 + qc
                self.dma("gpsimd", self.sST[b], self.d_glo_h[s, qg * 128 : (qg + 1) * 128, :],
                         stg_hb[:, b * D : (b + 1) * D])
                self.dma("gpsimd", self.sST[b], self.d_glo_l[s, qg * 128 : (qg + 1) * 128, :],
                         stg_lb[:, b * D : (b + 1) * D])

        self.wait("tensor", self.sV)
        self.wait("tensor", self.sS)
        guard_l = {}
        guard_p2 = {}
        guard_pv = {}
        pend = []
        for w in range(NW):
            for hp in range(8):
                for j in range(4):
                    lmm = {}
                    for par in (0, 1):
                        jb = (par * 2) + (j % 2)
                        if guard_l.get(jb):
                            self.wait("tensor", self.sS, guard_l[jb])
                    for pidx in range(3):
                        for par in (0, 1):
                            h = 2 * hp + par
                            bp = par * 64
                            fc = hp
                            jb = (par * 2) + (j % 2)
                            pl = pb[jb]
                            kc0 = fc * S + w * 256
                            xah = xfm_hb[bp : bp + 64, kc0 + j * 128 : kc0 + j * 128 + 128]
                            xal = xfm_lb[bp : bp + 64, kc0 + j * 128 : kc0 + j * 128 + 128]
                            xkh = xfm_hb[bp : bp + 64, kc0 : kc0 + 512]
                            xkl = xfm_lb[bp : bp + 64, kc0 : kc0 + 512]
                            ops = ((xah, xkh), (xah, xkl), (xal, xkh))[pidx]
                            self.mm(pl[:, :], ops[0], ops[1], start=(pidx == 0), stop=(pidx == 2),
                                    inc=(pidx == 2))
                            lmm[par] = self.sT.n
                    if j == 0 and pend:
                        for p_ in pend:
                            self._swa_pv(*p_)
                        pend = []
                    for par in (0, 1):
                        jb = (par * 2) + (j % 2)
                        slot = P2[:, (par * 4 + j) * 512 : (par * 4 + j) * 512 + 512]
                        if guard_p2.get(par):
                            self.wait("scalar", self.sT, guard_p2[par])
                        if j == 0:
                            self.wait("scalar", self.sV)
                        self.wait("scalar", self.sT, lmm[par])
                        self.act(slot, pb[jb][:, :], AF.Exp, scale=0.125,
                                 accum=srec[:, par * 4 + j : par * 4 + j + 1])
                        guard_l[jb] = self.sS.n
                expall = self.sS.n
                for par in (0, 1):
                    h = 2 * hp + par
                    rs = srec[:, 16 + par * 4 : 16 + par * 4 + 4]
                    self.wait("vector", self.sS, expall)
                    self.vrecip(rs, srec[:, par * 4 : par * 4 + 4])
                    self.vtt(rs, rs, self.twden[:, w * 4 : w * 4 + 4], OP.mult)
                    pend.append((s, w, h, par, rs, self.sV.n, expall, P2, x_tm, num, guard_pv, guard_p2))
            for p_ in pend:
                self._swa_pv(*p_)
            pend = []
            final = [2 * w, 2 * w + 1] + ([14, 15] if w == NW - 1 else [])
            for qg in final:
                slot = qg % 4
                b = slot % 2
                nsl = num[:, slot * D : (slot + 1) * D]
                self.wait("vector", self.sST[b])
                self.wait("vector", self.sTP)
                self.vcopy(stg_hb[:, b * D : (b + 1) * D], nsl)
                self.vtt(stg_lb[:, b * D : (b + 1) * D], nsl, stg_hb[:, b * D : (b + 1) * D], OP.subtract)
                self.wait("gpsimd", self.sV)
                self.dma("gpsimd", self.sST[b], self.d_loc_h[s, qg * 128 : (qg + 1) * 128, :],
                         stg_hb[:, b * D : (b + 1) * D])
                self.dma("gpsimd", self.sST[b], self.d_loc_l[s, qg * 128 : (qg + 1) * 128, :],
                         stg_lb[:, b * D : (b + 1) * D])

        ctb = al.slice(xfmstart, 16384)
        bcev = num
        self.wait("vector", self.sST[0])
        self.wait("vector", self.sST[1])
        self.barrier(("sync", "tensor", "vector"))
        ev_done = 0
        for t5 in range(4):
            b = t5 % 2
            if t5 >= 2:
                self.wait("sync", self.sT)
            self.dma("sync", self.sCT[b], ctb[:, b * 8192 : (b + 1) * 8192],
                     self.d_CT[t5])
            self.wait("tensor", self.sCT[b])
            for dch in range(8):
                bank = pb[4 + dch % 2]
                if dch >= 2 or t5 > 0:
                    self.wait("tensor", self.sV, ev_done)
                for sch in range(16):
                    self.mm(bank[:, :],
                            x_tm[:, sch * D + dch * 128 : sch * D + dch * 128 + 128],
                            ctb[:, b * 8192 + sch * 512 : b * 8192 + sch * 512 + 512],
                            start=(sch == 0), stop=(sch == 15), inc=(sch == 15))
                mm_n = self.sT.n
                self.wait("vector", self.sT, mm_n)
                if dch == 0:
                    self.wait("vector", self.sSB)
                self.vcopy(bcev[:, dch * 512 : dch * 512 + 512], bank[:, :])
                ev_done = self.sV.n
            self.wait("gpsimd", self.sV)
            self.dma("gpsimd", self.sSB,
                     self.d_bc[s, t5].rearrange("c p t -> p c t"),
                     bcev[:, 0:4096].rearrange("p (c t) -> p c t", t=512))

    def _cga_pv(self, tau, h, par, slot, expn, guard_pv, guard_p2, glob_ev, gtones):
        pv = self.pb[4 + par]
        if guard_pv.get(par):
            self.wait("tensor", self.sV, guard_pv[par])
        self.wait("tensor", self.sS, expn)
        for qc in range(4):
            self.mm(pv[:, qc * 65 : qc * 65 + 65],
                    slot[:, qc * 128 : qc * 128 + 128],
                    gtones[:, h * 65 : h * 65 + 65],
                    start=(qc == 0), stop=(qc == 3), inc=(qc == 3))
        guard_p2[par] = self.sT.n
        self.wait("vector", self.sT, self.sT.n)
        for qc in range(4):
            tmp = self.sc1[:, 10 + par : 11 + par]
            self.vrecip(tmp, pv[:, qc * 65 + 64 : qc * 65 + 65])
            self.vts(glob_ev[:, qc * D + h * 64 : qc * D + h * 64 + 64],
                     pv[:, qc * 65 : qc * 65 + 64], tmp, None, OP.mult)
        guard_pv[par] = self.sV.n

    def _swa_pv(self, s, w, h, par, rs, rn, expall, P2, x_tm, num, guard_pv, guard_p2):
        pv = self.pb[4 + par]
        if guard_pv.get(par):
            self.wait("tensor", self.sV, guard_pv[par])
        self.wait("tensor", self.sS, expall)
        for j in range(4):
            for kc in range(4):
                tv = 2 * w + kc
                self.mm(pv[:, j * 64 : j * 64 + 64],
                        P2[:, (par * 4 + kc) * 512 + j * 128 : (par * 4 + kc) * 512 + j * 128 + 128],
                        x_tm[:, tv * D + h * 64 : tv * D + h * 64 + 64],
                        start=(j == 0 and kc == 0), stop=(j == 3 and kc == 3),
                        inc=(j == 3 and kc == 3))
        guard_p2[par] = self.sT.n
        pvn = self.sT.n
        self.wait("vector", self.sT, pvn)
        self.wait("vector", self.sV, rn)
        for j in range(4):
            qg = 2 * w + j
            slot = qg % 4
            init = (w == 0) or (j >= 2)
            dst = num[:, slot * D + h * 64 : slot * D + h * 64 + 64]
            src = pv[:, j * 64 : j * 64 + 64]
            if init:
                self.vts(dst, src, rs[:, j : j + 1], None, OP.mult)
            else:
                self.vstt(dst, src, rs[:, j : j + 1], dst, OP.mult, OP.add)
        guard_pv[par] = self.sV.n

    def p15_prologue(self):
        al = self.al
        self.barrier()
        al.pos = self.base_pos
        self.mixw_h = al.take(8192)
        self.mixw_l = al.take(8192)
        self.outw_h = al.take(4096)
        self.outw_l = al.take(4096)
        self.p15_pos = al.pos
        stg = al.take(1024)
        mwh = self.mixw_h.bitcast(BF16)
        mwl = self.mixw_l.bitcast(BF16)
        owh = self.outw_h.bitcast(BF16)
        owl = self.outw_l.bitcast(BF16)
        for (dram, hh, ll, nch) in ((self.d_mixw, mwh, mwl, 16), (self.d_outw, owh, owl, 8)):
            for c in range(nch):
                self.wait("sync", self.sV)
                self.dma("sync", self.sW, stg, dram[c * 128 : (c + 1) * 128, :])
                self.wait("vector", self.sW)
                self.vcopy(hh[:, c * D : (c + 1) * D], stg)
                self.vtt(ll[:, c * D : (c + 1) * D], stg, hh[:, c * D : (c + 1) * D], OP.subtract)
        al.pos = self.p15_pos

    def phase15(self, s):
        self.mark(f'phase15({s})')
        al = self.al
        al.pos = self.p15_pos
        lgf_lh = al.take(2048)
        lgf_ll = al.take(2048)
        lgf_gh = al.take(2048)
        lgf_gl = al.take(2048)
        bcf = al.take(4096)
        xres = al.take(4096)
        gate = al.take(1024)
        mxh = al.take(2048)
        mxl = al.take(2048)
        y1 = al.take(1024)
        lns = al.take(1024)
        lnv = al.take(16)
        x1t = al.take(1024)
        lhb = lgf_lh.bitcast(BF16)
        llb = lgf_ll.bitcast(BF16)
        ghb = lgf_gh.bitcast(BF16)
        glb = lgf_gl.bitcast(BF16)
        mxhb = mxh.bitcast(BF16)
        mxlb = mxl.bitcast(BF16)
        mwh = self.mixw_h.bitcast(BF16)
        mwl = self.mixw_l.bitcast(BF16)
        owh = self.outw_h.bitcast(BF16)
        owl = self.outw_l.bitcast(BF16)
        pb = self.pb
        self.barrier()

        def p15_loads(tau, mm_guard, chain_guard):
            t0 = tau * 512
            if mm_guard:
                self.wait("scalar", self.sT, mm_guard)
                self.wait("scalar", self.sV, chain_guard)
            for (dram, dst) in ((self.d_loc_h[s], lhb), (self.d_loc_l[s], llb),
                                (self.d_glo_h[s], ghb), (self.d_glo_l[s], glb)):
                self.dmatp(dst.rearrange("f (c t) -> f c t", t=512),
                           dram[t0 : t0 + 512, :].rearrange("t (c f) -> t c f", f=128))
            if chain_guard:
                self.wait("sync", self.sV, chain_guard)
            self.dma("sync", self.sBCF, bcf.rearrange("p (c t) -> p c t", t=512),
                     self.d_bc[s, tau].rearrange("c p t -> p c t"))
            return (self.sBCF.n, self.sTP.n)

        def p15_xres(tau, part, guard):
            if guard:
                self.wait("sync", self.sV, guard)
            self.dma("sync", self.sXR[part],
                     xres[:, part * 2048 : (part + 1) * 2048].rearrange("p (c f) -> p c f", f=D),
                     self.d_x[s].rearrange("(c p) f -> p c f", p=128)[:, 4 * tau + 2 * part : 4 * tau + 2 * part + 2, :])
            return self.sXR[part].n

        next_ld = p15_loads(0, 0, 0)
        xres_ld = {0: p15_xres(0, 0, 0), 1: p15_xres(0, 1, 0)}
        for tau in range(4):
            ld_n, tp_n = next_ld
            self.wait("tensor", self.sBCF, ld_n)
            self.wait("tensor", self.sTP, tp_n)
            sig_done = {}
            chain_done = {}
            for och in range(8):
                pg = pb[och % 2]
                if sig_done.get(och % 2):
                    self.wait("tensor", self.sS, sig_done[och % 2])
                for hch in range(16):
                    if hch < 8:
                        ah = lhb[:, hch * 512 : hch * 512 + 512]
                        alo = llb[:, hch * 512 : hch * 512 + 512]
                    else:
                        ah = ghb[:, (hch - 8) * 512 : (hch - 8) * 512 + 512]
                        alo = glb[:, (hch - 8) * 512 : (hch - 8) * 512 + 512]
                    wh = mwh[:, hch * D + och * 128 : hch * D + och * 128 + 128]
                    wl = mwl[:, hch * D + och * 128 : hch * D + och * 128 + 128]
                    self.mm(pg[:, :], wh, ah, hch == 0, False)
                    self.mm(pg[:, :], wh, alo, False, False)
                    self.mm(pg[:, :], wl, ah, False, hch == 15, inc=(hch == 15))
                g_n = self.sT.n
                gslot = gate[:, (och % 2) * 512 : (och % 2) * 512 + 512]
                self.wait("scalar", self.sT, g_n)
                if chain_done.get(och % 2):
                    self.wait("scalar", self.sV, chain_done[och % 2])
                self.act(gslot, pg[:, :], AF.Sigmoid, bias=self.mixb[:, och : och + 1])
                sig_done[och % 2] = self.sS.n
                sg_n = self.sS.n
                lh_ = lhb[:, och * 512 : och * 512 + 512]
                ll_ = llb[:, och * 512 : och * 512 + 512]
                gh_ = ghb[:, och * 512 : och * 512 + 512]
                gl_ = glb[:, och * 512 : och * 512 + 512]
                t1 = lns[:, 0:512]
                t2 = lns[:, 512:1024]
                self.wait("vector", self.sS, sg_n)
                self.vtt(t1, lh_, ll_, OP.add)
                self.vtt(t2, gh_, gl_, OP.add)
                self.vtt(t1, t1, t2, OP.subtract)
                self.vtt(gslot, gslot, t1, OP.mult)
                self.vtt(gslot, gslot, t2, OP.add)
                self.vtt(gslot, gslot, bcf[:, och * 512 : och * 512 + 512], OP.add)
                self.vcopy(mxhb[:, och * 512 : och * 512 + 512], gslot)
                self.vtt(mxlb[:, och * 512 : och * 512 + 512], gslot,
                         mxhb[:, och * 512 : och * 512 + 512], OP.subtract)
                chain_done[och % 2] = self.sV.n
            mx_n = self.sV.n
            if tau < 3:
                next_ld = p15_loads(tau + 1, self.sT.n, mx_n)
            self.wait("tensor", self.sV, mx_n)
            for ts in range(4):
                tg = 4 * tau + ts
                po0, po1 = pb[2], pb[3]
                if getattr(self, "_y1ev", 0):
                    self.wait("tensor", self.sV, self._y1ev)
                for hch in range(8):
                    mh = mxhb[:, hch * 512 + ts * 128 : hch * 512 + ts * 128 + 128]
                    mlo = mxlb[:, hch * 512 + ts * 128 : hch * 512 + ts * 128 + 128]
                    for (pbk, ocol) in ((po0, 0), (po1, 512)):
                        wh = owh[:, hch * D + ocol : hch * D + ocol + 512]
                        wl = owl[:, hch * D + ocol : hch * D + ocol + 512]
                        self.mm(pbk[:, :], mh, wh, hch == 0, False)
                        self.mm(pbk[:, :], mlo, wh, False, False)
                        self.mm(pbk[:, :], mh, wl, False, hch == 7,
                                inc=(hch == 7 and ocol == 512))
                o_n = self.sT.n
                self.wait("vector", self.sT, o_n)
                if getattr(self, "_ln1s", 0):
                    self.wait("vector", self.sS, self._ln1s)
                for (pbk, ocol) in ((po0, 0), (po1, 512)):
                    self.vtt(y1[:, ocol : ocol + 512], pbk[:, :],
                             self.bias_bc[:, ocol : ocol + 512], OP.add)
                self.wait("vector", self.sXR[ts // 2], xres_ld[ts // 2])
                self.vtt(y1, y1, xres[:, ts * D : (ts + 1) * D], OP.add)
                y1_n = self.sV.n
                self._y1ev = y1_n
                if tau < 3 and ts == 1:
                    xres_ld[0] = p15_xres(tau + 1, 0, y1_n)
                if tau < 3 and ts == 3:
                    xres_ld[1] = p15_xres(tau + 1, 1, y1_n)
                self.wait("scalar", self.sV, y1_n)
                self.act(lns, y1, AF.Square, accum=lnv[:, 0:1])
                sq_n = self.sS.n
                self.vred(lnv[:, 1:2], y1, OP.add)
                self.wait("vector", self.sS, sq_n)
                self.wait("scalar", self.sX1)
                self._ln_tail(y1, lnv, x1t)
                self._ln1s = self.sS.n
                self.wait("vector", self.sS)
                self.vred(lnv[:, 8:9], x1t, OP.max, absv=True)
                self.vtt(self.gmax[:, 0:1], self.gmax[:, 0:1], lnv[:, 8:9], OP.max)
                self.wait("gpsimd", self.sS)
                self.dma("gpsimd", self.sX1, self.d_x1[s, tg], x1t)

    def _ln_tail(self, y, lnv, xout):
        nm = lnv[:, 2:3]
        var = lnv[:, 3:4]
        r = lnv[:, 4:5]
        t1 = lnv[:, 5:6]
        t2 = lnv[:, 6:7]
        self.vts(nm, lnv[:, 1:2], -1.0 / D, None, OP.mult)
        self.vtt(t1, nm, nm, OP.mult)
        self.vstt(var, lnv[:, 0:1], 1.0 / D, t1, OP.mult, OP.subtract)
        self.vts(var, var, LNEPS, None, OP.add)
        self.wait("scalar", self.sV)
        self.ssqrt(t1, var)
        self.wait("vector", self.sS)
        self.vrecip(r, t1)
        for _ in range(2):
            self.vtt(t1, r, r, OP.mult)
            self.vtt(t1, t1, var, OP.mult)
            self.vts(t1, t1, -0.5, 1.5, OP.mult, OP.add)
            self.vtt(r, r, t1, OP.mult)
        self.vtt(t2, nm, r, OP.mult)
        self.wait("scalar", self.sV)
        self.act(xout, y, AF.Identity, bias=t2, scale=r)

    def ffn_weight_prep(self, dram, dstT32, scales):
        al = self.al
        al.pos = self.ffn_pos
        dstT = dstT32.bitcast(BF16)
        stg = al.take(2048)
        qf16 = al.take(512)
        qb = al.take(1024)
        f16v = qf16.bitcast(F16)
        qbb = qb.bitcast(BF16)
        amax = al.take(2)
        rw = al.take(2)
        q_done = {}
        for e in ("sync", "gpsimd"):
            self.wait(e, self.sV)
            self.wait(e, self.sTP)
            self.wait(e, self.sW)
        for c in range(22):
            b = c % 2
            st = stg[:, b * D : (b + 1) * D]
            qs = qbb[:, b * D : (b + 1) * D]
            leng, lsem = (("sync", self.sCT[0]), ("gpsimd", self.sST[1]))[b]
            teng, tsem = (("scalar", self.sTP), ("sync", self.sW))[b]
            if q_done.get(b):
                self.wait(leng, self.sV, q_done[b])
            self.dma(leng, lsem, st, dram[c * 128 : (c + 1) * 128, :])
            self.wait("vector", lsem)
            self.vred(amax[:, 0:1], st, OP.max, absv=True)
            self.vts(scales[:, c : c + 1], amax[:, 0:1], 1.0 / 127.0, None, OP.mult)
            self.vrecip(rw[:, 0:1], amax[:, 0:1])
            self.vts(rw[:, 1:2], rw[:, 0:1], 127.0, None, OP.mult)
            self.wait("scalar", self.sV)
            self.act(f16v[:, 0:D], st, AF.Identity, bias=self.p_rnd[:, 0:1], scale=rw[:, 1:2])
            self.wait("vector", self.sS)
            self.wait("vector", tsem)
            self.vts(qs, f16v[:, 0:D], RND, None, OP.subtract)
            q_done[b] = self.sV.n
            self.wait(teng, self.sV)
            self.dmatp(dstT.rearrange("h (c i) -> h c i", i=IFF)[:, :, c * 128 : (c + 1) * 128],
                       qs.rearrange("i (c h) -> i c h", h=128), eng=teng, sem=tsem)

    def wd_weight_prep(self):
        al = self.al
        al.pos = self.ffn_pos
        dstT = self.wdqT.bitcast(BF16)
        stg = al.take(2 * IFF)
        qf16 = al.take(IFF // 2)
        qb = al.take(IFF)
        f16v = qf16.bitcast(F16)
        qbb = qb.bitcast(BF16)
        amax = al.take(2)
        rw = al.take(2)
        q_done = {}
        for e in ("sync", "gpsimd"):
            self.wait(e, self.sV)
            self.wait(e, self.sTP)
            self.wait(e, self.sW)
        for c in range(8):
            b = c % 2
            st = stg[:, b * IFF : (b + 1) * IFF]
            qs = qbb[:, b * IFF : (b + 1) * IFF]
            leng, lsem = (("sync", self.sCT[0]), ("gpsimd", self.sST[1]))[b]
            teng, tsem = (("scalar", self.sTP), ("sync", self.sW))[b]
            if q_done.get(b):
                self.wait(leng, self.sV, q_done[b])
            self.dma(leng, lsem, st, self.d_wd[c * 128 : (c + 1) * 128, :])
            self.wait("vector", lsem)
            self.vred(amax[:, 0:1], st, OP.max, absv=True)
            self.vts(self.swd[:, c : c + 1], amax[:, 0:1], 1.0 / 127.0, None, OP.mult)
            self.vrecip(rw[:, 0:1], amax[:, 0:1])
            self.vts(rw[:, 1:2], rw[:, 0:1], 127.0, None, OP.mult)
            self.wait("scalar", self.sV)
            self.act(f16v[:, 0:IFF], st, AF.Identity, bias=self.p_rnd[:, 0:1], scale=rw[:, 1:2])
            self.wait("vector", self.sS)
            self.wait("vector", tsem)
            self.vts(qs, f16v[:, 0:IFF], RND, None, OP.subtract)
            q_done[b] = self.sV.n
            self.wait(teng, self.sV)
            self.dmatp(dstT.rearrange("i (c o) -> i c o", o=D)[:, :, c * 128 : (c + 1) * 128],
                       qs.rearrange("o (c i) -> o c i", i=128), eng=teng, sem=tsem)

    def collective(self, idx):
        self.barrier()
        pb6 = self.pb[6]
        self.tp(pb6[0:1, 0:128], self.gmax[:, idx : idx + 1])
        self.wait("vector", self.sT)
        self.vred(self.sc1[0:1, 4:5], pb6[0:1, 0:128], OP.max)
        self.wait("gpsimd", self.sV)
        self.dma("gpsimd", self.sCCD, self.d_cc_in.ap(), self.sc1[0:1, 4:5])
        self.wait("gpsimd", self.sCCD)
        self._raw("gpsimd", lambda e: e.collective_compute(
            "AllReduce", OP.max, replica_groups=[list(range(8))],
            ins=[self.d_cc_in.ap()], outs=[self.d_cc_out.ap()]), inc=self.sG)
        self.wait("gpsimd", self.sG)
        self.dma("gpsimd", self.sCCD, self.sc1[0:1, 5:6], self.d_cc_out.ap())
        self.wait("scalar", self.sCCD)
        self.act(self.sc1[0:1, 6:7], self.sc1[0:1, idx : idx + 1], AF.Sigmoid, bias=self.zero1[0:1, 0:1])
        self.wait("vector", self.sS)
        self.vtt(self.sc1[0:1, 7:8], self.sc1[0:1, 5:6], self.sc1[0:1, 6:7], OP.mult)
        self.vts(self.sc1[0:1, 8:9], self.sc1[0:1, 7:8], 1.0 / 127.0, None, OP.mult)
        self.vrecip(self.sc1[0:1, 9:10], self.sc1[0:1, 8:9])
        self.wait("tensor", self.sV)
        self.mm(pb6[:, 0:2], self.ones1[:, :], self.sc1[0:1, 8:10], True, True, inc=True)
        self.wait("vector", self.sT)
        self.vcopy(self.p_sc[:, 2 * idx : 2 * idx + 2], pb6[:, 0:2])
        if idx == 0:
            self.vts(self.swgx[:, 0:22], self.swg[:, 0:22], self.p_sc[:, 0:1], None, OP.mult)
            self.vts(self.swux[:, 0:22], self.swu[:, 0:22], self.p_sc[:, 0:1], None, OP.mult)
        else:
            self.build_swd_bc()
        self.barrier()

    def build_swd_bc(self):
        pb7 = self.pb[7]
        self.wait("gpsimd", self.sCCD)
        self.dma("gpsimd", self.sCCD, self.d_swdscr.ap().rearrange("c p -> p c"),
                 self.swd[:, 0:8], slow=True)
        self.wait("sync", self.sCCD)
        self.dma("sync", self.sW, self.swd_bc[0:1, 0:D],
                 self.d_swdscr.ap().rearrange("c p -> (c p)")[None, :])
        self.wait("tensor", self.sW)
        for k in range(2):
            self.mm(pb7[:, 0:512], self.ones1[:, :], self.swd_bc[0:1, k * 512 : k * 512 + 512],
                    True, True, inc=True)
            self.wait("vector", self.sT)
            self.vts(self.swd_bc[:, k * 512 : k * 512 + 512], pb7[:, 0:512],
                     self.p_sc[:, 2:3], None, OP.mult)
            self.wait("tensor", self.sV)

    def phase2(self, s):
        self.mark(f'phase2({s})')
        al = self.al
        al.pos = self.ffn_pos
        x1q = al.take(8192)
        x1r = al.take(4096)
        qf16 = al.take(512)
        qtm = al.take(512)
        gbuf = al.take(512)
        ubuf = al.take(512)
        hbuf = al.take(1024)
        hmx = al.take(2)
        x1qb = x1q.bitcast(BF16)
        f16v = qf16.bitcast(F16)
        qtmb = qtm.bitcast(BF16)
        wg = self.wgqT.bitcast(BF16)
        wu = self.wupqT.bitcast(BF16)
        pb = self.pb
        self.barrier()
        for tau in range(4):
            self.wait("sync", self.sV)
            self.dma("sync", self.sW, x1r.rearrange("p (c f) -> p c f", f=D),
                     self.d_x1[s, 4 * tau : 4 * tau + 4].rearrange("c p f -> p c f"))
            ld_n = self.sW.n
            for tc in range(4):
                xt = x1r[:, tc * D : (tc + 1) * D]
                self.wait("vector", self.sW, ld_n)
                self.vts(f16v[:, 0:D], xt, self.p_sc[:, 1:2], RND, OP.mult, OP.add)
                self.wait("vector", self.sTP)
                self.vts(qtmb[:, 0:D], f16v[:, 0:D], RND, None, OP.subtract)
                self.vts(qtmb[:, 0:D], qtmb[:, 0:D], 127.0, -128.0, OP.min, OP.max)
                self.wait("scalar", self.sV)
                gtc = 4 * tau + tc
                self.dmatp(x1qb.rearrange("f (c t) -> f c t", t=S)[:, :, gtc * 128 : gtc * 128 + 128],
                           qtmb[:, 0:D].rearrange("t (c f) -> t c f", f=128))
        self.wait("tensor", self.sTP)
        if s == 0 and DEBUG:
            self.wait("gpsimd", self.sTP)
            self.dma("gpsimd", self.sSP, self.d_dbg1, x1qb[:, 0:16384])
        ev_guard = {}
        hspill = {}
        for tau in range(4):
            for ic in range(22):
                pg = pb[0 + 2 * (ic % 2)]
                pu = pb[1 + 2 * (ic % 2)]
                gk = ev_guard.get(ic % 2)
                if gk:
                    self.wait("tensor", self.sV, gk[0])
                    self.wait("tensor", self.sS, gk[1])
                for hch in range(8):
                    self.mm(pg[:, :], wg[:, hch * IFF + ic * 128 : hch * IFF + ic * 128 + 128],
                            x1qb[:, hch * S + tau * 512 : hch * S + tau * 512 + 512],
                            hch == 0, hch == 7, inc=(hch == 7))
                for hch in range(8):
                    self.mm(pu[:, :], wu[:, hch * IFF + ic * 128 : hch * IFF + ic * 128 + 128],
                            x1qb[:, hch * S + tau * 512 : hch * S + tau * 512 + 512],
                            hch == 0, hch == 7, inc=(hch == 7))
                mm_n = self.sT.n
                hslot = hbuf[:, (ic % 2) * 512 : (ic % 2) * 512 + 512]
                self.wait("scalar", self.sT, mm_n)
                self.wait("scalar", self.sV)
                self.act(gbuf, pg[:, :], AF.Silu, scale=self.swgx[:, ic : ic + 1])
                self.act(ubuf, pu[:, :], AF.Identity, bias=self.zero1[:, 0:1],
                         scale=self.swux[:, ic : ic + 1])
                sg_n = self.sS.n
                self.wait("vector", self.sS, sg_n)
                if hspill.get(ic % 2):
                    self.wait("vector", (self.sH, self.sH1)[ic % 2], hspill[ic % 2])
                self.vtt(hslot, gbuf, ubuf, OP.mult)
                self.vred(hmx[:, 0:1], hslot, OP.max, absv=True)
                self.vtt(self.gmax[:, 1:2], self.gmax[:, 1:2], hmx[:, 0:1], OP.max)
                self.wait("gpsimd", self.sV)
                hsem = (self.sH, self.sH1)[ic % 2]
                self.dma("gpsimd", hsem, self.d_h[s, ic, :, tau * 512 : (tau + 1) * 512], hslot)
                hspill[ic % 2] = hsem.n
                ev_guard[ic % 2] = (self.sV.n, self.sS.n)

    def phase3(self, s):
        self.mark(f'phase3({s})')
        al = self.al
        al.pos = self.wd_pos
        hq = al.take(11264)
        hrel = al.take(11264)
        qf16 = al.take(2816)
        x1r = al.take(8192)
        y2 = al.take(1024)
        lns = al.take(1024)
        lnv = al.take(16)
        x2t = al.take(2048)
        hqb = hq.bitcast(BF16)
        f16v = qf16.bitcast(F16)
        wd = self.wdqT.bitcast(BF16)
        pb = self.pb
        self.barrier()
        hq_mm = {}
        quant_done = {}
        half_ld = {}
        hload_guard = {}
        def p3_loads(tau, y2guard):
            xslot = (tau % 2) * 4096
            if y2guard:
                self.wait("sync", self.sV, y2guard)
            self.dma("sync", self.sXR[tau % 2], x1r[:, xslot : xslot + 4096].rearrange("p (c f) -> p c f", f=D),
                     self.d_x1[s, 4 * tau : 4 * tau + 4].rearrange("c p f -> p c f"))
            for half in range(2):
                hs = half * 5632
                hcs = half * 11
                if hload_guard.get(half):
                    self.wait("sync", self.sV, hload_guard[half])
                self.dma("sync", self.sCT[half],
                         hrel[:, hs : hs + 5632].rearrange("p (c t) -> p c t", t=512),
                         self.d_h[s][hcs : hcs + 11, :, tau * 512 : (tau + 1) * 512].rearrange("c p t -> p c t"))
            return self.sXR[tau % 2].n

        x1_ld = {0: p3_loads(0, 0)}
        for tau in range(4):
            hb = (tau % 2) * 11264
            xslot = (tau % 2) * 4096
            ld_x1 = x1_ld[tau]
            if tau >= 2:
                self.wait("vector", self.sT, hq_mm[tau % 2])
            for half in range(2):
                hs = half * 5632
                self.wait("vector", self.sCT[half])
                self.vts(f16v[:, 0:5632], hrel[:, hs : hs + 5632], self.p_sc[:, 3:4], RND,
                         OP.mult, OP.add)
                self.vts(hqb[:, hb + hs : hb + hs + 5632], f16v[:, 0:5632], RND, None, OP.subtract)
                self.vts(hqb[:, hb + hs : hb + hs + 5632], hqb[:, hb + hs : hb + hs + 5632],
                         127.0, -128.0, OP.min, OP.max)
                hload_guard[half] = self.sV.n
            hq_n = self.sV.n
            quant_done[tau] = hq_n
            if tau < 3:
                x1_ld[tau + 1] = p3_loads(tau + 1, getattr(self, "_y2ev", 0) if tau >= 1 else 0)
            self.wait("tensor", self.sV, hq_n)
            self.wait("vector", self.sXR[tau % 2], ld_x1)
            for ts in range(4):
                tg = 4 * tau + ts
                po0, po1 = pb[4], pb[5]
                if getattr(self, "_y2ev", 0):
                    self.wait("tensor", self.sV, self._y2ev)
                for ic in range(22):
                    lh = hqb[:, hb + ic * 512 + ts * 128 : hb + ic * 512 + ts * 128 + 128]
                    self.mm(po0[:, :], lh, wd[:, ic * D : ic * D + 512], ic == 0, ic == 21)
                    self.mm(po1[:, :], lh, wd[:, ic * D + 512 : ic * D + 1024], ic == 0, ic == 21,
                            inc=(ic == 21))
                o_n = self.sT.n
                hq_mm[tau % 2] = o_n
                self.wait("vector", self.sT, o_n)
                if getattr(self, "_ln2s", 0):
                    self.wait("vector", self.sS, self._ln2s)
                for (pbk, ocol) in ((po0, 0), (po1, 512)):
                    self.vtt(y2[:, ocol : ocol + 512], pbk[:, :],
                             self.swd_bc[:, ocol : ocol + 512], OP.mult)
                self.vtt(y2, y2, x1r[:, xslot + ts * D : xslot + (ts + 1) * D], OP.add)
                y2_n = self.sV.n
                self._y2ev = y2_n
                self.wait("scalar", self.sV, y2_n)
                self.act(lns, y2, AF.Square, accum=lnv[:, 0:1])
                sq_n = self.sS.n
                self.vred(lnv[:, 1:2], y2, OP.add)
                self.wait("vector", self.sS, sq_n)
                xb = (tg % 2) * D
                self.wait("vector", self.sOUT[tg % 2])
                self.wait("scalar", self.sOUT[tg % 2])
                self._ln_tail(y2, lnv, x2t[:, xb : xb + D])
                self._ln2s = self.sS.n
                self.wait("gpsimd", self.sS)
                self.dma("gpsimd", self.sOUT[tg % 2], self.d_out[s, tg * 128 : (tg + 1) * 128, :],
                         x2t[:, xb : xb + D])

    def finish(self):
        for sem in (self.sSP, self.sST[0], self.sST[1], self.sSB, self.sX1, self.sH, self.sH1,
                    self.sOUT[0], self.sOUT[1], self.sCCD, self.sBCF, self.sXR[0], self.sXR[1]):
            self.wait("gpsimd", sem)


_BUILT = None


def _build():
    global _BUILT
    if _BUILT is None:
        mk = MK()
        nc = mk.build()
        _BUILT = (mk, nc)
        if DEBUG:
            print(f"[mk] built: {mk.nins} ops, arena peak {mk.al.peak} f32 words "
                  f"({mk.al.peak * 4 // 1024} KiB/partition)")
    return _BUILT


def kernel(**inputs):
    CT, twden, ident, ones1 = _host_consts()
    x = np.ascontiguousarray(np.asarray(inputs["L0_attn_hidden_states"], dtype=np.float32))
    gm = np.asarray(inputs["L0_attn_global_memory"], dtype=np.float32)
    mem = np.ascontiguousarray(gm[0].transpose(1, 0, 2).reshape(G, D))
    cw = np.asarray(inputs["L0_attn_conv_w"], dtype=np.float32)
    cwT = np.ascontiguousarray(cw.transpose(1, 2, 0).reshape(D, 4 * D))
    convb = np.ascontiguousarray(np.asarray(inputs["L0_attn_conv_b"], np.float32).reshape(8, 128).T)
    mixb = np.ascontiguousarray(np.asarray(inputs["L0_attn_mix_b"], np.float32).reshape(8, 128).T)
    outb_bc = np.ascontiguousarray(
        np.broadcast_to(np.asarray(inputs["L0_attn_out_b"], np.float32)[None, :], (128, D)))
    clips = np.stack([np.asarray(inputs["L0_ffn_clip_x"], np.float32)[0],
                      np.asarray(inputs["L0_ffn_clip_down"], np.float32)[0]])[None, :]
    mk, nc = _build()
    common = {
        "gm_mem": mem,
        "cwT": cwT,
        "convb_pp": convb,
        "mix_w": np.ascontiguousarray(np.asarray(inputs["L0_attn_mix_w"], np.float32)),
        "mixb_pp": mixb,
        "out_w": np.ascontiguousarray(np.asarray(inputs["L0_attn_out_w"], np.float32)),
        "outb_bc": outb_bc,
        "clips": np.ascontiguousarray(clips),
        "w_gate": np.ascontiguousarray(np.asarray(inputs["L0_ffn_w_gate"], np.float32)),
        "w_up": np.ascontiguousarray(np.asarray(inputs["L0_ffn_w_up"], np.float32)),
        "w_down": np.ascontiguousarray(np.asarray(inputs["L0_ffn_w_down"], np.float32)),
        "CT": CT,
        "twden": twden,
        "ident": ident,
        "ones1": ones1,
    }
    in_maps = [dict(common, x_in=np.ascontiguousarray(x[2 * c : 2 * c + 2])) for c in range(8)]
    res = run_bass_kernel_spmd(nc, in_maps, core_ids=list(range(8)))
    out = np.concatenate([np.asarray(r["out"]) for r in res.results], axis=0)
    if DEBUG:
        kernel.last_results = res.results
    return out


# version 12
# speedup vs baseline: 1.0105x; 1.0105x over previous
import os
import math
import numpy as np
import ml_dtypes
import concourse.bass as bass
import concourse.mybir as mybir
from concourse.bass_utils import run_bass_kernel_spmd

F32 = mybir.dt.float32
F16 = mybir.dt.float16
BF16 = mybir.dt.bfloat16
AX = mybir.AxisListType.X
OP = mybir.AluOpType
AF = mybir.ActivationFunctionType

DEBUG = os.environ.get("MK_DEBUG", "") == "1"
TRUNC = os.environ.get("MK_TRUNC", "") == "1"

S = 2048
D = 1024
NH, HD = 16, 64
WIN, STR = 512, 256
NW = 7
G = 64
IFF = 2816
NSEQ = 2
NT = 16
RND = 1536.0
LNEPS = 1e-5
ENGS = ("sync", "tensor", "vector", "scalar", "gpsimd")


def _host_consts():
    cur = np.eye(S)
    res = np.zeros((S, S))
    s = 1
    while s < S:
        cur = cur + 0.5 * (np.roll(cur, s, axis=0) + np.roll(cur, -s, axis=0))
        res = res + cur
        s *= 2
    C = (res / (math.log2(S) + 1.0)).astype(np.float32)
    CT = C.T
    CTL = np.ascontiguousarray(CT.reshape(16, 128, 4, 512).transpose(2, 1, 0, 3))
    tri = np.linspace(0.5, 1.5, WIN)
    den = np.zeros(S)
    for w in range(NW):
        den[w * STR : w * STR + WIN] += tri
    denr = 1.0 / (den + 1e-6)
    twden = np.zeros((128, NW * 4), dtype=np.float32)
    for w in range(NW):
        for j in range(4):
            q = w * STR + j * 128 + np.arange(128)
            twden[:, w * 4 + j] = (tri[j * 128 : (j + 1) * 128] * denr[q]).astype(np.float32)
    ident = np.eye(128, dtype=np.float32)
    ones1 = np.ones((1, 128), dtype=np.float32)
    return CTL, twden, ident, ones1


class Sem:
    def __init__(self, nc, name):
        self.h = nc.alloc_semaphore(name)
        self.n = 0
        self.name = name


class Alloc:
    def __init__(self, tensor, total):
        self.t = tensor
        self.total = total
        self.pos = 0
        self.peak = 0

    def take(self, n, align=16):
        self.pos = (self.pos + align - 1) // align * align
        assert self.pos + n <= self.total, f"arena overflow: {self.pos}+{n} > {self.total}"
        ap = self.t[:, self.pos : self.pos + n]
        self.pos += n
        self.peak = max(self.peak, self.pos)
        return ap

    def slice(self, start, n):
        return self.t[:, start : start + n]


class MK:
    def __init__(self):
        nc = bass.Bass("TRN2", target_bir_lowering=False, debug=True)
        self.nc = nc
        spill_kind = "ExternalOutput" if DEBUG else "Internal"
        self.d_x = nc.dram_tensor("x_in", [NSEQ, S, D], F32, kind="ExternalInput").ap()
        self.d_gm = nc.dram_tensor("gm_mem", [G, D], F32, kind="ExternalInput").ap()
        self.d_cwT = nc.dram_tensor("cwT", [D, 4 * D], F32, kind="ExternalInput").ap()
        self.d_convb = nc.dram_tensor("convb_pp", [128, 8], F32, kind="ExternalInput").ap()
        self.d_mixw = nc.dram_tensor("mix_w", [2 * D, D], F32, kind="ExternalInput").ap()
        self.d_mixb = nc.dram_tensor("mixb_pp", [128, 8], F32, kind="ExternalInput").ap()
        self.d_outw = nc.dram_tensor("out_w", [D, D], F32, kind="ExternalInput").ap()
        self.d_outb_bc = nc.dram_tensor("outb_bc", [128, D], F32, kind="ExternalInput").ap()
        self.d_clip = nc.dram_tensor("clips", [1, 2], F32, kind="ExternalInput").ap()
        self.d_wg = nc.dram_tensor("w_gate", [IFF, D], F32, kind="ExternalInput").ap()
        self.d_wu = nc.dram_tensor("w_up", [IFF, D], F32, kind="ExternalInput").ap()
        self.d_wd = nc.dram_tensor("w_down", [D, IFF], F32, kind="ExternalInput").ap()
        self.d_CT = nc.dram_tensor("CT", [4, 128, 16, 512], F32, kind="ExternalInput").ap()
        self.d_twden = nc.dram_tensor("twden", [128, NW * 4], F32, kind="ExternalInput").ap()
        self.d_ident = nc.dram_tensor("ident", [128, 128], F32, kind="ExternalInput").ap()
        self.d_ones1 = nc.dram_tensor("ones1", [1, 128], F32, kind="ExternalInput").ap()
        self.d_out = nc.dram_tensor("out", [NSEQ, S, D], F32, kind="ExternalOutput").ap()
        self.d_loc_h = nc.dram_tensor("sp_loc_h", [NSEQ, S, D], BF16, kind=spill_kind).ap()
        self.d_loc_l = nc.dram_tensor("sp_loc_l", [NSEQ, S, D], BF16, kind=spill_kind).ap()
        self.d_glo_h = nc.dram_tensor("sp_glo_h", [NSEQ, S, D], BF16, kind=spill_kind).ap()
        self.d_glo_l = nc.dram_tensor("sp_glo_l", [NSEQ, S, D], BF16, kind=spill_kind).ap()
        self.d_bc = nc.dram_tensor("sp_bc", [NSEQ, 4, 8, 128, 512], F32, kind=spill_kind).ap()
        self.d_x1 = nc.dram_tensor("sp_x1", [NSEQ, NT, 128, D], F32, kind=spill_kind).ap()
        self.d_h = nc.dram_tensor("sp_h", [NSEQ, 22, 128, S], F32, kind=spill_kind).ap()
        self.d_dbg1 = nc.dram_tensor("dbg_x1q", [128, 16384], BF16, kind=spill_kind).ap()
        self.d_dbg2 = nc.dram_tensor("dbg_g", [128, 512], F32, kind=spill_kind).ap()
        self.d_cc_in = nc.dram_tensor("cc_in", [1, 1], F32)
        self.d_cc_out = nc.dram_tensor("cc_out", [1, 1], F32)
        self.d_swdscr = nc.dram_tensor("swdscr", [8, 128], F32)
        AFW = 52224
        self.arena = nc.alloc_sbuf_tensor("arena", [128, AFW], F32).ap()
        self.al = Alloc(self.arena, AFW)
        self.ident = nc.alloc_sbuf_tensor("identsb", [128, 128], F32).ap()
        self.ones1 = nc.alloc_sbuf_tensor("ones1sb", [1, 128], F32).ap()
        al = self.al
        self.twden = al.take(32)
        self.bias_bc = al.take(D)
        self.swd_bc = al.take(D)
        self.mixb = al.take(8)
        self.convb = al.take(8)
        self.p_sc = al.take(16)
        self.gmax = al.take(4)
        self.zero1 = al.take(4)
        self.p_rnd = al.take(4)
        self.swg = al.take(24)
        self.swu = al.take(24)
        self.swd = al.take(8)
        self.swgx = al.take(24)
        self.swux = al.take(24)
        self.sc1 = al.take(16)
        self.scr128 = al.take(128)
        self.base_pos = al.pos
        self.pb = [nc.alloc_psum_tensor(f"pb{i}", [128, 512], F32).ap() for i in range(8)]
        self.sX = Sem(nc, "sX")
        self.sW = Sem(nc, "sW")
        self.sCT = [Sem(nc, "sCT0"), Sem(nc, "sCT1")]
        self.sST = [Sem(nc, "sST0"), Sem(nc, "sST1")]
        self.sSB = Sem(nc, "sSB")
        self.sX1 = Sem(nc, "sX1")
        self.sH = Sem(nc, "sH")
        self.sH1 = Sem(nc, "sH1")
        self.sOUT = [Sem(nc, "sOUT0"), Sem(nc, "sOUT1")]
        self.sCCD = Sem(nc, "sCCD")
        self.sBCF = Sem(nc, "sBCF")
        self.sXR = [Sem(nc, "sXR0"), Sem(nc, "sXR1")]
        self.sTP = Sem(nc, "sTP")
        self.sSP = Sem(nc, "sSP")
        self.sT = Sem(nc, "sT")
        self.sV = Sem(nc, "sV")
        self.sS = Sem(nc, "sS")
        self.sG = Sem(nc, "sG")
        self.prog = {k: [] for k in ENGS}
        self.nins = 0
        self.marks = []
        self.cur_label = "init"
        self.itrace = {k: [] for k in ENGS}

    def mark(self, label):
        self.cur_label = label
        self.marks.append((self.nins, label, {k: len(v) for k, v in self.prog.items()}))

    def _raw(self, eng, fn, inc=None, k=1):
        self.itrace[eng].append(self.cur_label)
        self.nins += 1
        if inc is not None:
            inc.n += k
            self.prog[eng].append(lambda e, f=fn, s=inc.h, kk=k: f(e).then_inc(s, kk))
        else:
            self.prog[eng].append(lambda e, f=fn: f(e))

    def wait(self, eng, sem, val=None):
        v = sem.n if val is None else val
        if v is None or v <= 0:
            return
        self.itrace[eng].append("w:" + self.cur_label)
        self.nins += 1
        self.prog[eng].append(lambda e, s=sem.h, vv=v: e.wait_ge(s, vv))

    def dma(self, eng, sem, out, in_, slow=False):
        self._raw(eng, lambda e, o=out, i=in_, sl=slow:
                  e.dma_start(out=o, in_=i, allow_slow_non_contiguous=sl), inc=sem, k=16)

    def dmatp(self, out, in_):
        self._raw("scalar", lambda e, o=out, i=in_: e.dma_start_transpose(o, i), inc=self.sTP, k=16)

    def barrier(self, engs=ENGS):
        sems = [self.sX, self.sW, self.sTP, self.sSP, self.sT, self.sV, self.sS, self.sG,
                self.sCT[0], self.sCT[1], self.sST[0], self.sST[1], self.sSB, self.sX1,
                self.sH, self.sH1, self.sOUT[0], self.sOUT[1], self.sCCD, self.sBCF,
                self.sXR[0], self.sXR[1]]
        own = {"sync": (self.sX, self.sW, self.sCT[0], self.sCT[1], self.sBCF,
                        self.sXR[0], self.sXR[1]), "tensor": (self.sT,),
               "vector": (self.sV,), "scalar": (self.sS, self.sTP),
               "gpsimd": (self.sG, self.sSP, self.sST[0], self.sST[1], self.sSB, self.sX1,
                          self.sH, self.sH1, self.sOUT[0], self.sOUT[1], self.sCCD)}
        for e in engs:
            for s in sems:
                if s not in own[e] and s.n > 0:
                    self.wait(e, s)

    def _v(self, fn):
        self.wait("vector", self.sV)
        self._raw("vector", fn, inc=self.sV)

    def _s(self, fn):
        self.wait("scalar", self.sS)
        self._raw("scalar", fn, inc=self.sS)

    def mm(self, out, lhsT, rhs, start, stop, inc=False):
        self._raw("tensor", lambda e, o=out, a=lhsT, b=rhs, st=start, sp=stop:
                  e.matmul(o, a, b, start=st, stop=sp), inc=self.sT if inc else None)

    def tp(self, out, in_):
        self._raw("tensor", lambda e, o=out, i=in_, id_=self.ident: e.transpose(o, i, id_), inc=self.sT)

    def vcopy(self, out, in_):
        self._v(lambda e, o=out, i=in_: e.tensor_copy(o, i))

    def vtt(self, out, a, b, op):
        self._v(lambda e, o=out, x=a, y=b, p=op: e.tensor_tensor(o, x, y, p))

    def vts(self, out, in_, s1, s2, op1, op2=OP.bypass):
        self._v(lambda e, o=out, i=in_, a=s1, b=s2, p=op1, q=op2: e.tensor_scalar(o, i, a, b, p, q))

    def vstt(self, out, in0, scal, in1, op0, op1):
        self._v(lambda e, o=out, i=in0, s=scal, j=in1, p=op0, q=op1:
                e.scalar_tensor_tensor(o, i, s, j, p, q))

    def vred(self, out, in_, op, absv=False):
        self._v(lambda e, o=out, i=in_, p=op, a=absv:
                e.tensor_reduce(o, i, AX, p, apply_absolute_value=a))

    def vrecip(self, out, in_):
        self._v(lambda e, o=out, i=in_: e.reciprocal(o, i))

    def vmemset(self, ap, val):
        self._v(lambda e, o=ap, v=val: e.memset(o, v))

    def act(self, out, in_, func, bias=None, scale=1.0, accum=None):
        b = bias if bias is not None else self.zero1[:, 0:1]
        self._s(lambda e, o=out, i=in_, f=func, bb=b, sc=scale, ac=accum:
                e.activation(o, i, f, bias=bb, scale=sc, accum_out=ac))

    def ssqrt(self, out, in_):
        self._s(lambda e, o=out, i=in_: e.sqrt(o, i))

    def build(self):
        self.setup()
        for s in range(NSEQ):
            self.phase1(s)
        self.p15_prologue()
        for s in range(NSEQ):
            self.phase15(s)
        self.barrier()
        self.al.pos = self.base_pos
        self.wdqT = self.al.take(22 * D // 2)
        self.wd_pos = self.al.pos
        self.wgqT = self.al.take(IFF * 8 // 2)
        self.wupqT = self.al.take(IFF * 8 // 2)
        self.ffn_pos = self.al.pos
        self.ffn_weight_prep(self.d_wg, self.wgqT, self.swg)
        self.ffn_weight_prep(self.d_wu, self.wupqT, self.swu)
        self.wd_weight_prep()
        self.collective(0)
        if TRUNC:
            self.barrier()
            self.dma("gpsimd", self.sSP, self.d_dbg1, self.wgqT.bitcast(BF16)[:, 0:16384])
            self.finish()
            return self._materialize()
        for s in range(NSEQ):
            self.phase2(s)
        self.barrier()
        self.collective(1)
        for s in range(NSEQ):
            self.phase3(s)
        self.finish()
        return self._materialize()

    def _materialize(self):
        nc = self.nc
        with nc.Block() as block:
            @block.sync
            def _(e):
                for f in self.prog["sync"]:
                    f(e)
            @block.tensor
            def _(e):
                for f in self.prog["tensor"]:
                    f(e)
            @block.vector
            def _(e):
                for f in self.prog["vector"]:
                    f(e)
            @block.scalar
            def _(e):
                for f in self.prog["scalar"]:
                    f(e)
            @block.gpsimd
            def _(e):
                for f in self.prog["gpsimd"]:
                    f(e)
        return nc

    def setup(self):
        self.dma("sync", self.sX, self.ident[:, :], self.d_ident)
        self.dma("sync", self.sX, self.ones1[:, :], self.d_ones1)
        self.dma("sync", self.sX, self.twden[:, : NW * 4], self.d_twden)
        self.dma("sync", self.sX, self.bias_bc, self.d_outb_bc)
        self.dma("sync", self.sX, self.mixb, self.d_mixb)
        self.dma("sync", self.sX, self.convb, self.d_convb)
        self.dma("sync", self.sX, self.sc1[0:1, 0:2], self.d_clip)
        self.vmemset(self.zero1, 0.0)
        self.vmemset(self.gmax, 0.0)
        self.vmemset(self.p_rnd, RND)

    def phase1(self, s):
        self.mark(f'phase1({s})')
        al = self.al
        al.pos = self.base_pos
        x_tm = al.take(NT * D)
        p2start = al.pos
        P2 = al.take(4096)
        numstart = al.pos
        num = al.take(4 * D)
        xfmstart = al.pos
        xfm_h = al.take(8 * S // 2)
        xfm_l = al.take(8 * S // 2)
        stg_h = al.take(D)
        stg_l = al.take(D)
        gt_tm = al.take(D)
        gt_h = al.take(D // 2)
        gt_l = al.take(D // 2)
        gtones = al.take(NH * 65 + 7)
        srec = al.take(32)
        xfm_hb = xfm_h.bitcast(BF16)
        xfm_lb = xfm_l.bitcast(BF16)
        stg_hb = stg_h.bitcast(BF16)
        stg_lb = stg_l.bitcast(BF16)
        gt_hb = gt_h.bitcast(BF16)
        gt_lb = gt_l.bitcast(BF16)
        pb = self.pb

        if s > 0:
            self.barrier()
        self.dma("sync", self.sX,
                 x_tm[:, 0 : 8 * D].rearrange("p (c f) -> p c f", f=D),
                 self.d_x[s].rearrange("(c p) f -> p c f", p=128)[:, 0:8, :])
        ldx = self.sX.n
        self.dma("gpsimd", self.sSP,
                 x_tm[:, 8 * D : 16 * D].rearrange("p (c f) -> p c f", f=D),
                 self.d_x[s].rearrange("(c p) f -> p c f", p=128)[:, 8:16, :])
        ldx_b = self.sSP.n

        for tc in range(NT):
            b = tc % 2
            xs = x_tm[:, tc * D : (tc + 1) * D]
            sh = stg_hb[:, b * D : (b + 1) * D]
            sl = stg_lb[:, b * D : (b + 1) * D]
            if tc < 8:
                self.wait("vector", self.sX, ldx)
            else:
                self.wait("vector", self.sSP, ldx_b)
            if tc >= 2:
                self.wait("vector", self.sTP)
            self.vcopy(sh, xs)
            self.vtt(sl, xs, sh, OP.subtract)
            self.wait("scalar", self.sV)
            for (src, dstf) in ((sh, xfm_hb), (sl, xfm_lb)):
                self.dmatp(
                    dstf.rearrange("f (c t) -> f c t", t=S)[:, :, tc * 128 : (tc + 1) * 128],
                    src.rearrange("t (c f) -> t c f", f=128))
        tp_x = self.sTP.n

        self.mark('conv')
        x256 = al.take(2048)
        comp = al.take(512)
        cwb = al.slice(p2start, 8192)
        self.wait("tensor", self.sX, ldx)
        for i in range(16):
            tc, fch = i % 2, i // 2
            bank = pb[6 + i % 2]
            if i >= 2:
                self.wait("tensor", self.sV)
            self.tp(bank[:, 0:128], x_tm[:, tc * D + fch * 128 : tc * D + fch * 128 + 128])
            self.wait("vector", self.sT)
            self.vcopy(x256[:, fch * 256 + tc * 128 : fch * 256 + tc * 128 + 128], bank[:, 0:128])
        x256r = x256.rearrange("p (c t k) -> p c k t", k=4, t=64)
        self.wait("tensor", self.sV)
        nmm = 0
        cw_mm = {}
        for ich in range(8):
            cb = ich % 2
            if ich >= 2:
                self.wait("sync", self.sT, cw_mm[cb])
            self.dma("sync", self.sCT[cb], cwb[:, cb * 4096 : (cb + 1) * 4096],
                     self.d_cwT[ich * 128 : (ich + 1) * 128, :])
            self.wait("tensor", self.sCT[cb])
            for k in range(4):
                for och in range(8):
                    self.mm(pb[0][:, och * 64 : och * 64 + 64],
                            cwb[:, cb * 4096 + k * D + och * 128 : cb * 4096 + k * D + och * 128 + 128],
                            x256r[:, ich, k, :],
                            start=(nmm == 0), stop=(nmm == 255), inc=True)
                    nmm += 1
            cw_mm[cb] = self.sT.n
        self.wait("vector", self.sT)
        for och in range(8):
            self.vts(comp[:, och * 64 : och * 64 + 64], pb[0][:, och * 64 : och * 64 + 64],
                     self.convb[:, och : och + 1], None, OP.add)
        self.dma("sync", self.sX, gt_tm[64:128, :], self.d_gm)
        gm_n = self.sX.n
        self.wait("tensor", self.sV)
        for och in range(8):
            bank = pb[6 + och % 2]
            if och >= 2:
                self.wait("tensor", self.sV)
            self.tp(bank[0:64, 0:128], comp[:, och * 64 : och * 64 + 64])
            self.wait("vector", self.sT)
            self.vcopy(gt_tm[0:64, och * 128 : och * 128 + 128], bank[0:64, 0:128])
        self.wait("vector", self.sX, gm_n)
        self.wait("vector", self.sTP)
        self.vcopy(stg_hb[:, 0:D], gt_tm)
        self.vtt(stg_lb[:, 0:D], gt_tm, stg_hb[:, 0:D], OP.subtract)
        self.wait("scalar", self.sV)
        self.dmatp(gt_hb.rearrange("f (c k) -> f c k", k=128),
                   stg_hb[:, 0:D].rearrange("k (c f) -> k c f", f=128))
        self.dmatp(gt_lb.rearrange("f (c k) -> f c k", k=128),
                   stg_lb[:, 0:D].rearrange("k (c f) -> k c f", f=128))
        self.vmemset(gtones[:, : NH * 65], 1.0)
        for h in range(NH):
            self.vcopy(gtones[:, h * 65 : h * 65 + 64], gt_tm[:, h * 64 : h * 64 + 64])
        gt_v = self.sV.n
        gt_tp = self.sTP.n

        self.wait("tensor", self.sSP, ldx_b)
        self.mark('cga')
        glob_ev = num
        self.wait("tensor", self.sTP, tp_x)
        self.wait("tensor", self.sTP, gt_tp)
        self.wait("tensor", self.sV, gt_v)
        guard_l = {}
        guard_p2 = {}
        guard_pv = {}
        pend = []
        for tau in range(4):
            for hp in range(8):
                lmm = {}
                for par in (0, 1):
                    if guard_l.get(par):
                        self.wait("tensor", self.sS, guard_l[par])
                for pidx in range(3):
                    for par in (0, 1):
                        h = 2 * hp + par
                        pl = pb[par]
                        bp = par * 64
                        fc = hp
                        glh = gt_hb[bp : bp + 64, fc * 128 : fc * 128 + 128]
                        gll = gt_lb[bp : bp + 64, fc * 128 : fc * 128 + 128]
                        xrh = xfm_hb[bp : bp + 64, fc * S + tau * 512 : fc * S + tau * 512 + 512]
                        xrl = xfm_lb[bp : bp + 64, fc * S + tau * 512 : fc * S + tau * 512 + 512]
                        ops = ((glh, xrh), (glh, xrl), (gll, xrh))[pidx]
                        self.mm(pl[:, :], ops[0], ops[1], start=(pidx == 0), stop=(pidx == 2),
                                inc=(pidx == 2))
                        lmm[par] = self.sT.n
                for p_ in pend:
                    self._cga_pv(*p_)
                pend = []
                for par in (0, 1):
                    h = 2 * hp + par
                    slot = P2[:, par * 512 : par * 512 + 512]
                    if guard_p2.get(par):
                        self.wait("scalar", self.sT, guard_p2[par])
                    self.wait("scalar", self.sT, lmm[par])
                    self.act(slot, pb[par][:, :], AF.Exp, scale=0.125)
                    guard_l[par] = self.sS.n
                    pend.append((tau, h, par, slot, self.sS.n, guard_pv, guard_p2, glob_ev, gtones))
            for p_ in pend:
                self._cga_pv(*p_)
            pend = []
            for qc in range(4):
                b = qc % 2
                gl = glob_ev[:, qc * D : (qc + 1) * D]
                self.wait("vector", self.sST[b])
                self.wait("vector", self.sTP)
                self.vcopy(stg_hb[:, b * D : (b + 1) * D], gl)
                self.vtt(stg_lb[:, b * D : (b + 1) * D], gl, stg_hb[:, b * D : (b + 1) * D], OP.subtract)
                self.wait("gpsimd", self.sV)
                qg = tau * 4 + qc
                self.dma("gpsimd", self.sST[b], self.d_glo_h[s, qg * 128 : (qg + 1) * 128, :],
                         stg_hb[:, b * D : (b + 1) * D])
                self.dma("gpsimd", self.sST[b], self.d_glo_l[s, qg * 128 : (qg + 1) * 128, :],
                         stg_lb[:, b * D : (b + 1) * D])

        self.wait("tensor", self.sV)
        self.wait("tensor", self.sS)
        guard_l = {}
        guard_p2 = {}
        guard_pv = {}
        pend = []
        for w in range(NW):
            for hp in range(8):
                for j in range(4):
                    lmm = {}
                    for par in (0, 1):
                        jb = (par * 2) + (j % 2)
                        if guard_l.get(jb):
                            self.wait("tensor", self.sS, guard_l[jb])
                    for pidx in range(3):
                        for par in (0, 1):
                            h = 2 * hp + par
                            bp = par * 64
                            fc = hp
                            jb = (par * 2) + (j % 2)
                            pl = pb[jb]
                            kc0 = fc * S + w * 256
                            xah = xfm_hb[bp : bp + 64, kc0 + j * 128 : kc0 + j * 128 + 128]
                            xal = xfm_lb[bp : bp + 64, kc0 + j * 128 : kc0 + j * 128 + 128]
                            xkh = xfm_hb[bp : bp + 64, kc0 : kc0 + 512]
                            xkl = xfm_lb[bp : bp + 64, kc0 : kc0 + 512]
                            ops = ((xah, xkh), (xah, xkl), (xal, xkh))[pidx]
                            self.mm(pl[:, :], ops[0], ops[1], start=(pidx == 0), stop=(pidx == 2),
                                    inc=(pidx == 2))
                            lmm[par] = self.sT.n
                    if j == 0 and pend:
                        for p_ in pend:
                            self._swa_pv(*p_)
                        pend = []
                    for par in (0, 1):
                        jb = (par * 2) + (j % 2)
                        slot = P2[:, (par * 4 + j) * 512 : (par * 4 + j) * 512 + 512]
                        if guard_p2.get(par):
                            self.wait("scalar", self.sT, guard_p2[par])
                        if j == 0:
                            self.wait("scalar", self.sV)
                        self.wait("scalar", self.sT, lmm[par])
                        self.act(slot, pb[jb][:, :], AF.Exp, scale=0.125,
                                 accum=srec[:, par * 4 + j : par * 4 + j + 1])
                        guard_l[jb] = self.sS.n
                expall = self.sS.n
                for par in (0, 1):
                    h = 2 * hp + par
                    rs = srec[:, 16 + par * 4 : 16 + par * 4 + 4]
                    self.wait("vector", self.sS, expall)
                    self.vrecip(rs, srec[:, par * 4 : par * 4 + 4])
                    self.vtt(rs, rs, self.twden[:, w * 4 : w * 4 + 4], OP.mult)
                    pend.append((s, w, h, par, rs, self.sV.n, expall, P2, x_tm, num, guard_pv, guard_p2))
            for p_ in pend:
                self._swa_pv(*p_)
            pend = []
            final = [2 * w, 2 * w + 1] + ([14, 15] if w == NW - 1 else [])
            for qg in final:
                slot = qg % 4
                b = slot % 2
                nsl = num[:, slot * D : (slot + 1) * D]
                self.wait("vector", self.sST[b])
                self.wait("vector", self.sTP)
                self.vcopy(stg_hb[:, b * D : (b + 1) * D], nsl)
                self.vtt(stg_lb[:, b * D : (b + 1) * D], nsl, stg_hb[:, b * D : (b + 1) * D], OP.subtract)
                self.wait("gpsimd", self.sV)
                self.dma("gpsimd", self.sST[b], self.d_loc_h[s, qg * 128 : (qg + 1) * 128, :],
                         stg_hb[:, b * D : (b + 1) * D])
                self.dma("gpsimd", self.sST[b], self.d_loc_l[s, qg * 128 : (qg + 1) * 128, :],
                         stg_lb[:, b * D : (b + 1) * D])

        ctb = al.slice(xfmstart, 16384)
        bcev = num
        self.wait("vector", self.sST[0])
        self.wait("vector", self.sST[1])
        self.barrier(("sync", "tensor", "vector"))
        ev_done = 0
        for t5 in range(4):
            b = t5 % 2
            if t5 >= 2:
                self.wait("sync", self.sT)
            self.dma("sync", self.sCT[b], ctb[:, b * 8192 : (b + 1) * 8192],
                     self.d_CT[t5])
            self.wait("tensor", self.sCT[b])
            for dch in range(8):
                bank = pb[4 + dch % 2]
                if dch >= 2 or t5 > 0:
                    self.wait("tensor", self.sV, ev_done)
                for sch in range(16):
                    self.mm(bank[:, :],
                            x_tm[:, sch * D + dch * 128 : sch * D + dch * 128 + 128],
                            ctb[:, b * 8192 + sch * 512 : b * 8192 + sch * 512 + 512],
                            start=(sch == 0), stop=(sch == 15), inc=(sch == 15))
                mm_n = self.sT.n
                self.wait("vector", self.sT, mm_n)
                if dch == 0:
                    self.wait("vector", self.sSB)
                self.vcopy(bcev[:, dch * 512 : dch * 512 + 512], bank[:, :])
                ev_done = self.sV.n
            self.wait("gpsimd", self.sV)
            self.dma("gpsimd", self.sSB,
                     self.d_bc[s, t5].rearrange("c p t -> p c t"),
                     bcev[:, 0:4096].rearrange("p (c t) -> p c t", t=512))

    def _cga_pv(self, tau, h, par, slot, expn, guard_pv, guard_p2, glob_ev, gtones):
        pv = self.pb[4 + par]
        if guard_pv.get(par):
            self.wait("tensor", self.sV, guard_pv[par])
        self.wait("tensor", self.sS, expn)
        for qc in range(4):
            self.mm(pv[:, qc * 65 : qc * 65 + 65],
                    slot[:, qc * 128 : qc * 128 + 128],
                    gtones[:, h * 65 : h * 65 + 65],
                    start=(qc == 0), stop=(qc == 3), inc=(qc == 3))
        guard_p2[par] = self.sT.n
        self.wait("vector", self.sT, self.sT.n)
        for qc in range(4):
            tmp = self.sc1[:, 10 + par : 11 + par]
            self.vrecip(tmp, pv[:, qc * 65 + 64 : qc * 65 + 65])
            self.vts(glob_ev[:, qc * D + h * 64 : qc * D + h * 64 + 64],
                     pv[:, qc * 65 : qc * 65 + 64], tmp, None, OP.mult)
        guard_pv[par] = self.sV.n

    def _swa_pv(self, s, w, h, par, rs, rn, expall, P2, x_tm, num, guard_pv, guard_p2):
        pv = self.pb[4 + par]
        if guard_pv.get(par):
            self.wait("tensor", self.sV, guard_pv[par])
        self.wait("tensor", self.sS, expall)
        for j in range(4):
            for kc in range(4):
                tv = 2 * w + kc
                self.mm(pv[:, j * 64 : j * 64 + 64],
                        P2[:, (par * 4 + kc) * 512 + j * 128 : (par * 4 + kc) * 512 + j * 128 + 128],
                        x_tm[:, tv * D + h * 64 : tv * D + h * 64 + 64],
                        start=(j == 0 and kc == 0), stop=(j == 3 and kc == 3),
                        inc=(j == 3 and kc == 3))
        guard_p2[par] = self.sT.n
        pvn = self.sT.n
        self.wait("vector", self.sT, pvn)
        self.wait("vector", self.sV, rn)
        for j in range(4):
            qg = 2 * w + j
            slot = qg % 4
            init = (w == 0) or (j >= 2)
            dst = num[:, slot * D + h * 64 : slot * D + h * 64 + 64]
            src = pv[:, j * 64 : j * 64 + 64]
            if init:
                self.vts(dst, src, rs[:, j : j + 1], None, OP.mult)
            else:
                self.vstt(dst, src, rs[:, j : j + 1], dst, OP.mult, OP.add)
        guard_pv[par] = self.sV.n

    def p15_prologue(self):
        al = self.al
        self.barrier()
        al.pos = self.base_pos
        self.mixw_h = al.take(8192)
        self.mixw_l = al.take(8192)
        self.outw_h = al.take(4096)
        self.outw_l = al.take(4096)
        self.p15_pos = al.pos
        stg = al.take(1024)
        mwh = self.mixw_h.bitcast(BF16)
        mwl = self.mixw_l.bitcast(BF16)
        owh = self.outw_h.bitcast(BF16)
        owl = self.outw_l.bitcast(BF16)
        for (dram, hh, ll, nch) in ((self.d_mixw, mwh, mwl, 16), (self.d_outw, owh, owl, 8)):
            for c in range(nch):
                self.wait("sync", self.sV)
                self.dma("sync", self.sW, stg, dram[c * 128 : (c + 1) * 128, :])
                self.wait("vector", self.sW)
                self.vcopy(hh[:, c * D : (c + 1) * D], stg)
                self.vtt(ll[:, c * D : (c + 1) * D], stg, hh[:, c * D : (c + 1) * D], OP.subtract)
        al.pos = self.p15_pos

    def phase15(self, s):
        self.mark(f'phase15({s})')
        al = self.al
        al.pos = self.p15_pos
        lgf_lh = al.take(2048)
        lgf_ll = al.take(2048)
        lgf_gh = al.take(2048)
        lgf_gl = al.take(2048)
        bcf = al.take(4096)
        xres = al.take(4096)
        gate = al.take(1024)
        mxh = al.take(2048)
        mxl = al.take(2048)
        y1 = al.take(1024)
        lns = al.take(1024)
        lnv = al.take(16)
        x1t = al.take(1024)
        lhb = lgf_lh.bitcast(BF16)
        llb = lgf_ll.bitcast(BF16)
        ghb = lgf_gh.bitcast(BF16)
        glb = lgf_gl.bitcast(BF16)
        mxhb = mxh.bitcast(BF16)
        mxlb = mxl.bitcast(BF16)
        mwh = self.mixw_h.bitcast(BF16)
        mwl = self.mixw_l.bitcast(BF16)
        owh = self.outw_h.bitcast(BF16)
        owl = self.outw_l.bitcast(BF16)
        pb = self.pb
        self.barrier()

        def p15_loads(tau, mm_guard, chain_guard):
            t0 = tau * 512
            if mm_guard:
                self.wait("scalar", self.sT, mm_guard)
                self.wait("scalar", self.sV, chain_guard)
            for (dram, dst) in ((self.d_loc_h[s], lhb), (self.d_loc_l[s], llb),
                                (self.d_glo_h[s], ghb), (self.d_glo_l[s], glb)):
                self.dmatp(dst.rearrange("f (c t) -> f c t", t=512),
                           dram[t0 : t0 + 512, :].rearrange("t (c f) -> t c f", f=128))
            if chain_guard:
                self.wait("sync", self.sV, chain_guard)
            self.dma("sync", self.sBCF, bcf.rearrange("p (c t) -> p c t", t=512),
                     self.d_bc[s, tau].rearrange("c p t -> p c t"))
            return (self.sBCF.n, self.sTP.n)

        def p15_xres(tau, part, guard):
            if guard:
                self.wait("sync", self.sV, guard)
            self.dma("sync", self.sXR[part],
                     xres[:, part * 2048 : (part + 1) * 2048].rearrange("p (c f) -> p c f", f=D),
                     self.d_x[s].rearrange("(c p) f -> p c f", p=128)[:, 4 * tau + 2 * part : 4 * tau + 2 * part + 2, :])
            return self.sXR[part].n

        next_ld = p15_loads(0, 0, 0)
        xres_ld = {0: p15_xres(0, 0, 0), 1: p15_xres(0, 1, 0)}
        for tau in range(4):
            ld_n, tp_n = next_ld
            self.wait("tensor", self.sBCF, ld_n)
            self.wait("tensor", self.sTP, tp_n)
            sig_done = {}
            chain_done = {}
            for och in range(8):
                pg = pb[och % 2]
                if sig_done.get(och % 2):
                    self.wait("tensor", self.sS, sig_done[och % 2])
                for hch in range(16):
                    if hch < 8:
                        ah = lhb[:, hch * 512 : hch * 512 + 512]
                        alo = llb[:, hch * 512 : hch * 512 + 512]
                    else:
                        ah = ghb[:, (hch - 8) * 512 : (hch - 8) * 512 + 512]
                        alo = glb[:, (hch - 8) * 512 : (hch - 8) * 512 + 512]
                    wh = mwh[:, hch * D + och * 128 : hch * D + och * 128 + 128]
                    wl = mwl[:, hch * D + och * 128 : hch * D + och * 128 + 128]
                    self.mm(pg[:, :], wh, ah, hch == 0, False)
                    self.mm(pg[:, :], wh, alo, False, False)
                    self.mm(pg[:, :], wl, ah, False, hch == 15, inc=(hch == 15))
                g_n = self.sT.n
                gslot = gate[:, (och % 2) * 512 : (och % 2) * 512 + 512]
                self.wait("scalar", self.sT, g_n)
                if chain_done.get(och % 2):
                    self.wait("scalar", self.sV, chain_done[och % 2])
                self.act(gslot, pg[:, :], AF.Sigmoid, bias=self.mixb[:, och : och + 1])
                sig_done[och % 2] = self.sS.n
                sg_n = self.sS.n
                lh_ = lhb[:, och * 512 : och * 512 + 512]
                ll_ = llb[:, och * 512 : och * 512 + 512]
                gh_ = ghb[:, och * 512 : och * 512 + 512]
                gl_ = glb[:, och * 512 : och * 512 + 512]
                t1 = lns[:, 0:512]
                t2 = lns[:, 512:1024]
                self.wait("vector", self.sS, sg_n)
                self.vtt(t1, lh_, ll_, OP.add)
                self.vtt(t2, gh_, gl_, OP.add)
                self.vtt(t1, t1, t2, OP.subtract)
                self.vtt(gslot, gslot, t1, OP.mult)
                self.vtt(gslot, gslot, t2, OP.add)
                self.vtt(gslot, gslot, bcf[:, och * 512 : och * 512 + 512], OP.add)
                self.vcopy(mxhb[:, och * 512 : och * 512 + 512], gslot)
                self.vtt(mxlb[:, och * 512 : och * 512 + 512], gslot,
                         mxhb[:, och * 512 : och * 512 + 512], OP.subtract)
                chain_done[och % 2] = self.sV.n
            mx_n = self.sV.n
            if tau < 3:
                next_ld = p15_loads(tau + 1, self.sT.n, mx_n)
            self.wait("tensor", self.sV, mx_n)
            for ts in range(4):
                tg = 4 * tau + ts
                po0, po1 = pb[2], pb[3]
                if getattr(self, "_y1ev", 0):
                    self.wait("tensor", self.sV, self._y1ev)
                for hch in range(8):
                    mh = mxhb[:, hch * 512 + ts * 128 : hch * 512 + ts * 128 + 128]
                    mlo = mxlb[:, hch * 512 + ts * 128 : hch * 512 + ts * 128 + 128]
                    for (pbk, ocol) in ((po0, 0), (po1, 512)):
                        wh = owh[:, hch * D + ocol : hch * D + ocol + 512]
                        wl = owl[:, hch * D + ocol : hch * D + ocol + 512]
                        self.mm(pbk[:, :], mh, wh, hch == 0, False)
                        self.mm(pbk[:, :], mlo, wh, False, False)
                        self.mm(pbk[:, :], mh, wl, False, hch == 7,
                                inc=(hch == 7 and ocol == 512))
                o_n = self.sT.n
                self.wait("vector", self.sT, o_n)
                if getattr(self, "_ln1s", 0):
                    self.wait("vector", self.sS, self._ln1s)
                for (pbk, ocol) in ((po0, 0), (po1, 512)):
                    self.vtt(y1[:, ocol : ocol + 512], pbk[:, :],
                             self.bias_bc[:, ocol : ocol + 512], OP.add)
                self.wait("vector", self.sXR[ts // 2], xres_ld[ts // 2])
                self.vtt(y1, y1, xres[:, ts * D : (ts + 1) * D], OP.add)
                y1_n = self.sV.n
                self._y1ev = y1_n
                if tau < 3 and ts == 1:
                    xres_ld[0] = p15_xres(tau + 1, 0, y1_n)
                if tau < 3 and ts == 3:
                    xres_ld[1] = p15_xres(tau + 1, 1, y1_n)
                self.wait("scalar", self.sV, y1_n)
                self.act(lns, y1, AF.Square, accum=lnv[:, 0:1])
                sq_n = self.sS.n
                self.vred(lnv[:, 1:2], y1, OP.add)
                self.wait("vector", self.sS, sq_n)
                self.wait("scalar", self.sX1)
                self._ln_tail(y1, lnv, x1t)
                self._ln1s = self.sS.n
                self.wait("vector", self.sS)
                self.vred(lnv[:, 8:9], x1t, OP.max, absv=True)
                self.vtt(self.gmax[:, 0:1], self.gmax[:, 0:1], lnv[:, 8:9], OP.max)
                self.wait("gpsimd", self.sS)
                self.dma("gpsimd", self.sX1, self.d_x1[s, tg], x1t)

    def _ln_tail(self, y, lnv, xout):
        nm = lnv[:, 2:3]
        var = lnv[:, 3:4]
        r = lnv[:, 4:5]
        t1 = lnv[:, 5:6]
        t2 = lnv[:, 6:7]
        self.vts(nm, lnv[:, 1:2], -1.0 / D, None, OP.mult)
        self.vtt(t1, nm, nm, OP.mult)
        self.vstt(var, lnv[:, 0:1], 1.0 / D, t1, OP.mult, OP.subtract)
        self.vts(var, var, LNEPS, None, OP.add)
        self.wait("scalar", self.sV)
        self.ssqrt(t1, var)
        self.wait("vector", self.sS)
        self.vrecip(r, t1)
        for _ in range(2):
            self.vtt(t1, r, r, OP.mult)
            self.vtt(t1, t1, var, OP.mult)
            self.vts(t1, t1, -0.5, 1.5, OP.mult, OP.add)
            self.vtt(r, r, t1, OP.mult)
        self.vtt(t2, nm, r, OP.mult)
        self.wait("scalar", self.sV)
        self.act(xout, y, AF.Identity, bias=t2, scale=r)

    def ffn_weight_prep(self, dram, dstT32, scales):
        al = self.al
        al.pos = self.ffn_pos
        dstT = dstT32.bitcast(BF16)
        stg = al.take(2048)
        qf16 = al.take(512)
        qb = al.take(1024)
        f16v = qf16.bitcast(F16)
        qbb = qb.bitcast(BF16)
        amax = al.take(2)
        rw = al.take(2)
        q_done = {}
        self.wait("sync", self.sV)
        self.wait("sync", self.sTP)
        for c in range(22):
            b = c % 2
            st = stg[:, b * D : (b + 1) * D]
            qs = qbb[:, b * D : (b + 1) * D]
            if q_done.get(b):
                self.wait("sync", self.sV, q_done[b])
            self.dma("sync", self.sCT[b], st, dram[c * 128 : (c + 1) * 128, :])
            self.wait("vector", self.sCT[b])
            self.vred(amax[:, 0:1], st, OP.max, absv=True)
            self.vts(scales[:, c : c + 1], amax[:, 0:1], 1.0 / 127.0, None, OP.mult)
            self.vrecip(rw[:, 0:1], amax[:, 0:1])
            self.vts(rw[:, 1:2], rw[:, 0:1], 127.0, None, OP.mult)
            self.wait("scalar", self.sV)
            self.act(f16v[:, 0:D], st, AF.Identity, bias=self.p_rnd[:, 0:1], scale=rw[:, 1:2])
            self.wait("vector", self.sS)
            self.wait("vector", self.sTP)
            self.vts(qs, f16v[:, 0:D], RND, None, OP.subtract)
            q_done[b] = self.sV.n
            self.wait("scalar", self.sV)
            self.dmatp(dstT.rearrange("h (c i) -> h c i", i=IFF)[:, :, c * 128 : (c + 1) * 128],
                       qs.rearrange("i (c h) -> i c h", h=128))

    def wd_weight_prep(self):
        al = self.al
        al.pos = self.ffn_pos
        dstT = self.wdqT.bitcast(BF16)
        stg = al.take(2 * IFF)
        qf16 = al.take(IFF // 2)
        qb = al.take(IFF)
        f16v = qf16.bitcast(F16)
        qbb = qb.bitcast(BF16)
        amax = al.take(2)
        rw = al.take(2)
        q_done = {}
        self.wait("sync", self.sV)
        self.wait("sync", self.sTP)
        for c in range(8):
            b = c % 2
            st = stg[:, b * IFF : (b + 1) * IFF]
            qs = qbb[:, b * IFF : (b + 1) * IFF]
            if q_done.get(b):
                self.wait("sync", self.sV, q_done[b])
            self.dma("sync", self.sCT[b], st, self.d_wd[c * 128 : (c + 1) * 128, :])
            self.wait("vector", self.sCT[b])
            self.vred(amax[:, 0:1], st, OP.max, absv=True)
            self.vts(self.swd[:, c : c + 1], amax[:, 0:1], 1.0 / 127.0, None, OP.mult)
            self.vrecip(rw[:, 0:1], amax[:, 0:1])
            self.vts(rw[:, 1:2], rw[:, 0:1], 127.0, None, OP.mult)
            self.wait("scalar", self.sV)
            self.act(f16v[:, 0:IFF], st, AF.Identity, bias=self.p_rnd[:, 0:1], scale=rw[:, 1:2])
            self.wait("vector", self.sS)
            self.wait("vector", self.sTP)
            self.vts(qs, f16v[:, 0:IFF], RND, None, OP.subtract)
            q_done[b] = self.sV.n
            self.wait("scalar", self.sV)
            self.dmatp(dstT.rearrange("i (c o) -> i c o", o=D)[:, :, c * 128 : (c + 1) * 128],
                       qs.rearrange("o (c i) -> o c i", i=128))

    def collective(self, idx):
        self.barrier()
        pb6 = self.pb[6]
        self.tp(pb6[0:1, 0:128], self.gmax[:, idx : idx + 1])
        self.wait("vector", self.sT)
        self.vred(self.sc1[0:1, 4:5], pb6[0:1, 0:128], OP.max)
        self.wait("gpsimd", self.sV)
        self.dma("gpsimd", self.sCCD, self.d_cc_in.ap(), self.sc1[0:1, 4:5])
        self.wait("gpsimd", self.sCCD)
        self._raw("gpsimd", lambda e: e.collective_compute(
            "AllReduce", OP.max, replica_groups=[list(range(8))],
            ins=[self.d_cc_in.ap()], outs=[self.d_cc_out.ap()]), inc=self.sG)
        self.wait("gpsimd", self.sG)
        self.dma("gpsimd", self.sCCD, self.sc1[0:1, 5:6], self.d_cc_out.ap())
        self.wait("scalar", self.sCCD)
        self.act(self.sc1[0:1, 6:7], self.sc1[0:1, idx : idx + 1], AF.Sigmoid, bias=self.zero1[0:1, 0:1])
        self.wait("vector", self.sS)
        self.vtt(self.sc1[0:1, 7:8], self.sc1[0:1, 5:6], self.sc1[0:1, 6:7], OP.mult)
        self.vts(self.sc1[0:1, 8:9], self.sc1[0:1, 7:8], 1.0 / 127.0, None, OP.mult)
        self.vrecip(self.sc1[0:1, 9:10], self.sc1[0:1, 8:9])
        self.wait("tensor", self.sV)
        self.mm(pb6[:, 0:2], self.ones1[:, :], self.sc1[0:1, 8:10], True, True, inc=True)
        self.wait("vector", self.sT)
        self.vcopy(self.p_sc[:, 2 * idx : 2 * idx + 2], pb6[:, 0:2])
        if idx == 0:
            self.vts(self.swgx[:, 0:22], self.swg[:, 0:22], self.p_sc[:, 0:1], None, OP.mult)
            self.vts(self.swux[:, 0:22], self.swu[:, 0:22], self.p_sc[:, 0:1], None, OP.mult)
        else:
            self.build_swd_bc()
        self.barrier()

    def build_swd_bc(self):
        pb7 = self.pb[7]
        self.wait("gpsimd", self.sCCD)
        self.dma("gpsimd", self.sCCD, self.d_swdscr.ap().rearrange("c p -> p c"),
                 self.swd[:, 0:8], slow=True)
        self.wait("sync", self.sCCD)
        self.dma("sync", self.sW, self.swd_bc[0:1, 0:D],
                 self.d_swdscr.ap().rearrange("c p -> (c p)")[None, :])
        self.wait("tensor", self.sW)
        for k in range(2):
            self.mm(pb7[:, 0:512], self.ones1[:, :], self.swd_bc[0:1, k * 512 : k * 512 + 512],
                    True, True, inc=True)
            self.wait("vector", self.sT)
            self.vts(self.swd_bc[:, k * 512 : k * 512 + 512], pb7[:, 0:512],
                     self.p_sc[:, 2:3], None, OP.mult)
            self.wait("tensor", self.sV)

    def phase2(self, s):
        self.mark(f'phase2({s})')
        al = self.al
        al.pos = self.ffn_pos
        x1q = al.take(8192)
        x1r = al.take(4096)
        qf16 = al.take(512)
        qtm = al.take(512)
        gbuf = al.take(512)
        ubuf = al.take(512)
        hbuf = al.take(1024)
        hmx = al.take(2)
        x1qb = x1q.bitcast(BF16)
        f16v = qf16.bitcast(F16)
        qtmb = qtm.bitcast(BF16)
        wg = self.wgqT.bitcast(BF16)
        wu = self.wupqT.bitcast(BF16)
        pb = self.pb
        self.barrier()
        for tau in range(4):
            self.wait("sync", self.sV)
            self.dma("sync", self.sW, x1r.rearrange("p (c f) -> p c f", f=D),
                     self.d_x1[s, 4 * tau : 4 * tau + 4].rearrange("c p f -> p c f"))
            ld_n = self.sW.n
            for tc in range(4):
                xt = x1r[:, tc * D : (tc + 1) * D]
                self.wait("vector", self.sW, ld_n)
                self.vts(f16v[:, 0:D], xt, self.p_sc[:, 1:2], RND, OP.mult, OP.add)
                self.wait("vector", self.sTP)
                self.vts(qtmb[:, 0:D], f16v[:, 0:D], RND, None, OP.subtract)
                self.vts(qtmb[:, 0:D], qtmb[:, 0:D], 127.0, -128.0, OP.min, OP.max)
                self.wait("scalar", self.sV)
                gtc = 4 * tau + tc
                self.dmatp(x1qb.rearrange("f (c t) -> f c t", t=S)[:, :, gtc * 128 : gtc * 128 + 128],
                           qtmb[:, 0:D].rearrange("t (c f) -> t c f", f=128))
        self.wait("tensor", self.sTP)
        if s == 0 and DEBUG:
            self.wait("gpsimd", self.sTP)
            self.dma("gpsimd", self.sSP, self.d_dbg1, x1qb[:, 0:16384])
        ev_guard = {}
        hspill = {}
        for tau in range(4):
            for ic in range(22):
                pg = pb[0 + 2 * (ic % 2)]
                pu = pb[1 + 2 * (ic % 2)]
                gk = ev_guard.get(ic % 2)
                if gk:
                    self.wait("tensor", self.sV, gk[0])
                    self.wait("tensor", self.sS, gk[1])
                for hch in range(8):
                    self.mm(pg[:, :], wg[:, hch * IFF + ic * 128 : hch * IFF + ic * 128 + 128],
                            x1qb[:, hch * S + tau * 512 : hch * S + tau * 512 + 512],
                            hch == 0, hch == 7, inc=(hch == 7))
                for hch in range(8):
                    self.mm(pu[:, :], wu[:, hch * IFF + ic * 128 : hch * IFF + ic * 128 + 128],
                            x1qb[:, hch * S + tau * 512 : hch * S + tau * 512 + 512],
                            hch == 0, hch == 7, inc=(hch == 7))
                mm_n = self.sT.n
                hslot = hbuf[:, (ic % 2) * 512 : (ic % 2) * 512 + 512]
                self.wait("scalar", self.sT, mm_n)
                self.wait("scalar", self.sV)
                self.act(gbuf, pg[:, :], AF.Silu, scale=self.swgx[:, ic : ic + 1])
                self.act(ubuf, pu[:, :], AF.Identity, bias=self.zero1[:, 0:1],
                         scale=self.swux[:, ic : ic + 1])
                sg_n = self.sS.n
                self.wait("vector", self.sS, sg_n)
                if hspill.get(ic % 2):
                    self.wait("vector", (self.sH, self.sH1)[ic % 2], hspill[ic % 2])
                self.vtt(hslot, gbuf, ubuf, OP.mult)
                self.vred(hmx[:, 0:1], hslot, OP.max, absv=True)
                self.vtt(self.gmax[:, 1:2], self.gmax[:, 1:2], hmx[:, 0:1], OP.max)
                self.wait("gpsimd", self.sV)
                hsem = (self.sH, self.sH1)[ic % 2]
                self.dma("gpsimd", hsem, self.d_h[s, ic, :, tau * 512 : (tau + 1) * 512], hslot)
                hspill[ic % 2] = hsem.n
                ev_guard[ic % 2] = (self.sV.n, self.sS.n)

    def phase3(self, s):
        self.mark(f'phase3({s})')
        al = self.al
        al.pos = self.wd_pos
        hq = al.take(11264)
        hrel = al.take(11264)
        qf16 = al.take(2816)
        x1r = al.take(8192)
        y2 = al.take(1024)
        lns = al.take(1024)
        lnv = al.take(16)
        x2t = al.take(2048)
        hqb = hq.bitcast(BF16)
        f16v = qf16.bitcast(F16)
        wd = self.wdqT.bitcast(BF16)
        pb = self.pb
        self.barrier()
        hq_mm = {}
        quant_done = {}
        half_ld = {}
        hload_guard = {}
        def p3_loads(tau, y2guard):
            xslot = (tau % 2) * 4096
            if y2guard:
                self.wait("sync", self.sV, y2guard)
            self.dma("sync", self.sXR[tau % 2], x1r[:, xslot : xslot + 4096].rearrange("p (c f) -> p c f", f=D),
                     self.d_x1[s, 4 * tau : 4 * tau + 4].rearrange("c p f -> p c f"))
            for half in range(2):
                hs = half * 5632
                hcs = half * 11
                if hload_guard.get(half):
                    self.wait("sync", self.sV, hload_guard[half])
                self.dma("sync", self.sCT[half],
                         hrel[:, hs : hs + 5632].rearrange("p (c t) -> p c t", t=512),
                         self.d_h[s][hcs : hcs + 11, :, tau * 512 : (tau + 1) * 512].rearrange("c p t -> p c t"))
            return self.sXR[tau % 2].n

        x1_ld = {0: p3_loads(0, 0)}
        for tau in range(4):
            hb = (tau % 2) * 11264
            xslot = (tau % 2) * 4096
            ld_x1 = x1_ld[tau]
            if tau >= 2:
                self.wait("vector", self.sT, hq_mm[tau % 2])
            for half in range(2):
                hs = half * 5632
                self.wait("vector", self.sCT[half])
                self.vts(f16v[:, 0:5632], hrel[:, hs : hs + 5632], self.p_sc[:, 3:4], RND,
                         OP.mult, OP.add)
                self.vts(hqb[:, hb + hs : hb + hs + 5632], f16v[:, 0:5632], RND, None, OP.subtract)
                self.vts(hqb[:, hb + hs : hb + hs + 5632], hqb[:, hb + hs : hb + hs + 5632],
                         127.0, -128.0, OP.min, OP.max)
                hload_guard[half] = self.sV.n
            hq_n = self.sV.n
            quant_done[tau] = hq_n
            if tau < 3:
                x1_ld[tau + 1] = p3_loads(tau + 1, getattr(self, "_y2ev", 0) if tau >= 1 else 0)
            self.wait("tensor", self.sV, hq_n)
            self.wait("vector", self.sXR[tau % 2], ld_x1)
            for ts in range(4):
                tg = 4 * tau + ts
                po0, po1 = pb[4], pb[5]
                if getattr(self, "_y2ev", 0):
                    self.wait("tensor", self.sV, self._y2ev)
                for ic in range(22):
                    lh = hqb[:, hb + ic * 512 + ts * 128 : hb + ic * 512 + ts * 128 + 128]
                    self.mm(po0[:, :], lh, wd[:, ic * D : ic * D + 512], ic == 0, ic == 21)
                    self.mm(po1[:, :], lh, wd[:, ic * D + 512 : ic * D + 1024], ic == 0, ic == 21,
                            inc=(ic == 21))
                o_n = self.sT.n
                hq_mm[tau % 2] = o_n
                self.wait("vector", self.sT, o_n)
                if getattr(self, "_ln2s", 0):
                    self.wait("vector", self.sS, self._ln2s)
                for (pbk, ocol) in ((po0, 0), (po1, 512)):
                    self.vtt(y2[:, ocol : ocol + 512], pbk[:, :],
                             self.swd_bc[:, ocol : ocol + 512], OP.mult)
                self.vtt(y2, y2, x1r[:, xslot + ts * D : xslot + (ts + 1) * D], OP.add)
                y2_n = self.sV.n
                self._y2ev = y2_n
                self.wait("scalar", self.sV, y2_n)
                self.act(lns, y2, AF.Square, accum=lnv[:, 0:1])
                sq_n = self.sS.n
                self.vred(lnv[:, 1:2], y2, OP.add)
                self.wait("vector", self.sS, sq_n)
                xb = (tg % 2) * D
                self.wait("vector", self.sOUT[tg % 2])
                self.wait("scalar", self.sOUT[tg % 2])
                self._ln_tail(y2, lnv, x2t[:, xb : xb + D])
                self._ln2s = self.sS.n
                self.wait("gpsimd", self.sS)
                self.dma("gpsimd", self.sOUT[tg % 2], self.d_out[s, tg * 128 : (tg + 1) * 128, :],
                         x2t[:, xb : xb + D])

    def finish(self):
        for sem in (self.sSP, self.sST[0], self.sST[1], self.sSB, self.sX1, self.sH, self.sH1,
                    self.sOUT[0], self.sOUT[1], self.sCCD, self.sBCF, self.sXR[0], self.sXR[1]):
            self.wait("gpsimd", sem)


_BUILT = None


def _build():
    global _BUILT
    if _BUILT is None:
        mk = MK()
        nc = mk.build()
        _BUILT = (mk, nc)
        if DEBUG:
            print(f"[mk] built: {mk.nins} ops, arena peak {mk.al.peak} f32 words "
                  f"({mk.al.peak * 4 // 1024} KiB/partition)")
    return _BUILT


def kernel(**inputs):
    CT, twden, ident, ones1 = _host_consts()
    x = np.ascontiguousarray(np.asarray(inputs["L0_attn_hidden_states"], dtype=np.float32))
    gm = np.asarray(inputs["L0_attn_global_memory"], dtype=np.float32)
    mem = np.ascontiguousarray(gm[0].transpose(1, 0, 2).reshape(G, D))
    cw = np.asarray(inputs["L0_attn_conv_w"], dtype=np.float32)
    cwT = np.ascontiguousarray(cw.transpose(1, 2, 0).reshape(D, 4 * D))
    convb = np.ascontiguousarray(np.asarray(inputs["L0_attn_conv_b"], np.float32).reshape(8, 128).T)
    mixb = np.ascontiguousarray(np.asarray(inputs["L0_attn_mix_b"], np.float32).reshape(8, 128).T)
    outb_bc = np.ascontiguousarray(
        np.broadcast_to(np.asarray(inputs["L0_attn_out_b"], np.float32)[None, :], (128, D)))
    clips = np.stack([np.asarray(inputs["L0_ffn_clip_x"], np.float32)[0],
                      np.asarray(inputs["L0_ffn_clip_down"], np.float32)[0]])[None, :]
    mk, nc = _build()
    common = {
        "gm_mem": mem,
        "cwT": cwT,
        "convb_pp": convb,
        "mix_w": np.ascontiguousarray(np.asarray(inputs["L0_attn_mix_w"], np.float32)),
        "mixb_pp": mixb,
        "out_w": np.ascontiguousarray(np.asarray(inputs["L0_attn_out_w"], np.float32)),
        "outb_bc": outb_bc,
        "clips": np.ascontiguousarray(clips),
        "w_gate": np.ascontiguousarray(np.asarray(inputs["L0_ffn_w_gate"], np.float32)),
        "w_up": np.ascontiguousarray(np.asarray(inputs["L0_ffn_w_up"], np.float32)),
        "w_down": np.ascontiguousarray(np.asarray(inputs["L0_ffn_w_down"], np.float32)),
        "CT": CT,
        "twden": twden,
        "ident": ident,
        "ones1": ones1,
    }
    in_maps = [dict(common, x_in=np.ascontiguousarray(x[2 * c : 2 * c + 2])) for c in range(8)]
    res = run_bass_kernel_spmd(nc, in_maps, core_ids=list(range(8)))
    out = np.concatenate([np.asarray(r["out"]) for r in res.results], axis=0)
    if DEBUG:
        kernel.last_results = res.results
    return out


# version 13 (final)
# speedup vs baseline: 1.0185x; 1.0079x over previous
import os
import math
import numpy as np
import ml_dtypes
import concourse.bass as bass
import concourse.mybir as mybir
from concourse.bass_utils import run_bass_kernel_spmd

F32 = mybir.dt.float32
F16 = mybir.dt.float16
BF16 = mybir.dt.bfloat16
AX = mybir.AxisListType.X
OP = mybir.AluOpType
AF = mybir.ActivationFunctionType

DEBUG = os.environ.get("MK_DEBUG", "") == "1"
TRUNC = os.environ.get("MK_TRUNC", "") == "1"

S = 2048
D = 1024
NH, HD = 16, 64
WIN, STR = 512, 256
NW = 7
G = 64
IFF = 2816
NSEQ = 2
NT = 16
RND = 1536.0
LNEPS = 1e-5
ENGS = ("sync", "tensor", "vector", "scalar", "gpsimd")


def _host_consts():
    cur = np.eye(S)
    res = np.zeros((S, S))
    s = 1
    while s < S:
        cur = cur + 0.5 * (np.roll(cur, s, axis=0) + np.roll(cur, -s, axis=0))
        res = res + cur
        s *= 2
    C = (res / (math.log2(S) + 1.0)).astype(np.float32)
    CT = C.T
    CTL = np.ascontiguousarray(CT.reshape(16, 128, 4, 512).transpose(2, 1, 0, 3))
    tri = np.linspace(0.5, 1.5, WIN)
    den = np.zeros(S)
    for w in range(NW):
        den[w * STR : w * STR + WIN] += tri
    denr = 1.0 / (den + 1e-6)
    twden = np.zeros((128, NW * 4), dtype=np.float32)
    for w in range(NW):
        for j in range(4):
            q = w * STR + j * 128 + np.arange(128)
            twden[:, w * 4 + j] = (tri[j * 128 : (j + 1) * 128] * denr[q]).astype(np.float32)
    ident = np.eye(128, dtype=np.float32)
    ones1 = np.ones((1, 128), dtype=np.float32)
    return CTL, twden, ident, ones1


class Sem:
    def __init__(self, nc, name):
        self.h = nc.alloc_semaphore(name)
        self.n = 0
        self.name = name


class Alloc:
    def __init__(self, tensor, total):
        self.t = tensor
        self.total = total
        self.pos = 0
        self.peak = 0

    def take(self, n, align=16):
        self.pos = (self.pos + align - 1) // align * align
        assert self.pos + n <= self.total, f"arena overflow: {self.pos}+{n} > {self.total}"
        ap = self.t[:, self.pos : self.pos + n]
        self.pos += n
        self.peak = max(self.peak, self.pos)
        return ap

    def slice(self, start, n):
        return self.t[:, start : start + n]


class MK:
    def __init__(self):
        nc = bass.Bass("TRN2", target_bir_lowering=False, debug=True)
        self.nc = nc
        spill_kind = "ExternalOutput" if DEBUG else "Internal"
        self.d_x = nc.dram_tensor("x_in", [NSEQ, S, D], F32, kind="ExternalInput").ap()
        self.d_gm = nc.dram_tensor("gm_mem", [G, D], F32, kind="ExternalInput").ap()
        self.d_cwT = nc.dram_tensor("cwT", [D, 4 * D], F32, kind="ExternalInput").ap()
        self.d_convb = nc.dram_tensor("convb_pp", [128, 8], F32, kind="ExternalInput").ap()
        self.d_mixw = nc.dram_tensor("mix_w", [2 * D, D], F32, kind="ExternalInput").ap()
        self.d_mixb = nc.dram_tensor("mixb_pp", [128, 8], F32, kind="ExternalInput").ap()
        self.d_outw = nc.dram_tensor("out_w", [D, D], F32, kind="ExternalInput").ap()
        self.d_outb_bc = nc.dram_tensor("outb_bc", [128, D], F32, kind="ExternalInput").ap()
        self.d_clip = nc.dram_tensor("clips", [1, 2], F32, kind="ExternalInput").ap()
        self.d_wg = nc.dram_tensor("w_gate", [IFF, D], F32, kind="ExternalInput").ap()
        self.d_wu = nc.dram_tensor("w_up", [IFF, D], F32, kind="ExternalInput").ap()
        self.d_wd = nc.dram_tensor("w_down", [D, IFF], F32, kind="ExternalInput").ap()
        self.d_CT = nc.dram_tensor("CT", [4, 128, 16, 512], F32, kind="ExternalInput").ap()
        self.d_twden = nc.dram_tensor("twden", [128, NW * 4], F32, kind="ExternalInput").ap()
        self.d_ident = nc.dram_tensor("ident", [128, 128], F32, kind="ExternalInput").ap()
        self.d_ones1 = nc.dram_tensor("ones1", [1, 128], F32, kind="ExternalInput").ap()
        self.d_out = nc.dram_tensor("out", [NSEQ, S, D], F32, kind="ExternalOutput").ap()
        self.d_loc_h = nc.dram_tensor("sp_loc_h", [NSEQ, S, D], BF16, kind=spill_kind).ap()
        self.d_loc_l = nc.dram_tensor("sp_loc_l", [NSEQ, S, D], BF16, kind=spill_kind).ap()
        self.d_glo_h = nc.dram_tensor("sp_glo_h", [NSEQ, S, D], BF16, kind=spill_kind).ap()
        self.d_glo_l = nc.dram_tensor("sp_glo_l", [NSEQ, S, D], BF16, kind=spill_kind).ap()
        self.d_bc = nc.dram_tensor("sp_bc", [NSEQ, 4, 8, 128, 512], F32, kind=spill_kind).ap()
        self.d_x1 = nc.dram_tensor("sp_x1", [NSEQ, NT, 128, D], F32, kind=spill_kind).ap()
        self.d_h = nc.dram_tensor("sp_h", [NSEQ, 22, 128, S], F32, kind=spill_kind).ap()
        self.d_dbg1 = nc.dram_tensor("dbg_x1q", [128, 16384], BF16, kind=spill_kind).ap()
        self.d_dbg2 = nc.dram_tensor("dbg_g", [128, 512], F32, kind=spill_kind).ap()
        self.d_cc_in = nc.dram_tensor("cc_in", [1, 1], F32)
        self.d_cc_out = nc.dram_tensor("cc_out", [1, 1], F32)
        self.d_swdscr = nc.dram_tensor("swdscr", [8, 128], F32)
        AFW = 52224
        self.arena = nc.alloc_sbuf_tensor("arena", [128, AFW], F32).ap()
        self.al = Alloc(self.arena, AFW)
        self.ident = nc.alloc_sbuf_tensor("identsb", [128, 128], F32).ap()
        self.ones1 = nc.alloc_sbuf_tensor("ones1sb", [1, 128], F32).ap()
        al = self.al
        self.twden = al.take(32)
        self.bias_bc = al.take(D)
        self.swd_bc = al.take(D)
        self.mixb = al.take(8)
        self.convb = al.take(8)
        self.p_sc = al.take(16)
        self.gmax = al.take(4)
        self.zero1 = al.take(4)
        self.p_rnd = al.take(4)
        self.swg = al.take(24)
        self.swu = al.take(24)
        self.swd = al.take(8)
        self.swgx = al.take(24)
        self.swux = al.take(24)
        self.sc1 = al.take(16)
        self.scr128 = al.take(128)
        self.base_pos = al.pos
        self.pb = [nc.alloc_psum_tensor(f"pb{i}", [128, 512], F32).ap() for i in range(8)]
        self.sX = Sem(nc, "sX")
        self.sW = Sem(nc, "sW")
        self.sCT = [Sem(nc, "sCT0"), Sem(nc, "sCT1")]
        self.sST = [Sem(nc, "sST0"), Sem(nc, "sST1")]
        self.sSB = Sem(nc, "sSB")
        self.sX1 = Sem(nc, "sX1")
        self.sH = Sem(nc, "sH")
        self.sH1 = Sem(nc, "sH1")
        self.sOUT = [Sem(nc, "sOUT0"), Sem(nc, "sOUT1")]
        self.sCCD = Sem(nc, "sCCD")
        self.sBCF = Sem(nc, "sBCF")
        self.sXR = [Sem(nc, "sXR0"), Sem(nc, "sXR1")]
        self.sTP = Sem(nc, "sTP")
        self.sSP = Sem(nc, "sSP")
        self.sT = Sem(nc, "sT")
        self.sV = Sem(nc, "sV")
        self.sS = Sem(nc, "sS")
        self.sG = Sem(nc, "sG")
        self.prog = {k: [] for k in ENGS}
        self.nins = 0
        self.marks = []
        self.cur_label = "init"
        self.itrace = {k: [] for k in ENGS}

    def mark(self, label):
        self.cur_label = label
        self.marks.append((self.nins, label, {k: len(v) for k, v in self.prog.items()}))

    def _raw(self, eng, fn, inc=None, k=1):
        self.itrace[eng].append(self.cur_label)
        self.nins += 1
        if inc is not None:
            inc.n += k
            self.prog[eng].append(lambda e, f=fn, s=inc.h, kk=k: f(e).then_inc(s, kk))
        else:
            self.prog[eng].append(lambda e, f=fn: f(e))

    def wait(self, eng, sem, val=None):
        v = sem.n if val is None else val
        if v is None or v <= 0:
            return
        self.itrace[eng].append("w:" + self.cur_label)
        self.nins += 1
        self.prog[eng].append(lambda e, s=sem.h, vv=v: e.wait_ge(s, vv))

    def dma(self, eng, sem, out, in_, slow=False):
        self._raw(eng, lambda e, o=out, i=in_, sl=slow:
                  e.dma_start(out=o, in_=i, allow_slow_non_contiguous=sl), inc=sem, k=16)

    def dmatp(self, out, in_):
        self._raw("scalar", lambda e, o=out, i=in_: e.dma_start_transpose(o, i), inc=self.sTP, k=16)

    def barrier(self, engs=ENGS):
        sems = [self.sX, self.sW, self.sTP, self.sSP, self.sT, self.sV, self.sS, self.sG,
                self.sCT[0], self.sCT[1], self.sST[0], self.sST[1], self.sSB, self.sX1,
                self.sH, self.sH1, self.sOUT[0], self.sOUT[1], self.sCCD, self.sBCF,
                self.sXR[0], self.sXR[1]]
        own = {"sync": (self.sX, self.sW, self.sCT[0], self.sCT[1], self.sBCF,
                        self.sXR[0], self.sXR[1]), "tensor": (self.sT,),
               "vector": (self.sV,), "scalar": (self.sS, self.sTP),
               "gpsimd": (self.sG, self.sSP, self.sST[0], self.sST[1], self.sSB, self.sX1,
                          self.sH, self.sH1, self.sOUT[0], self.sOUT[1], self.sCCD)}
        for e in engs:
            for s in sems:
                if s not in own[e] and s.n > 0:
                    self.wait(e, s)

    def _v(self, fn):
        self.wait("vector", self.sV)
        self._raw("vector", fn, inc=self.sV)

    def _s(self, fn):
        self.wait("scalar", self.sS)
        self._raw("scalar", fn, inc=self.sS)

    def mm(self, out, lhsT, rhs, start, stop, inc=False):
        self._raw("tensor", lambda e, o=out, a=lhsT, b=rhs, st=start, sp=stop:
                  e.matmul(o, a, b, start=st, stop=sp), inc=self.sT if inc else None)

    def tp(self, out, in_):
        self._raw("tensor", lambda e, o=out, i=in_, id_=self.ident: e.transpose(o, i, id_), inc=self.sT)

    def vcopy(self, out, in_):
        self._v(lambda e, o=out, i=in_: e.tensor_copy(o, i))

    def vtt(self, out, a, b, op):
        self._v(lambda e, o=out, x=a, y=b, p=op: e.tensor_tensor(o, x, y, p))

    def vts(self, out, in_, s1, s2, op1, op2=OP.bypass):
        self._v(lambda e, o=out, i=in_, a=s1, b=s2, p=op1, q=op2: e.tensor_scalar(o, i, a, b, p, q))

    def vstt(self, out, in0, scal, in1, op0, op1):
        self._v(lambda e, o=out, i=in0, s=scal, j=in1, p=op0, q=op1:
                e.scalar_tensor_tensor(o, i, s, j, p, q))

    def vred(self, out, in_, op, absv=False):
        self._v(lambda e, o=out, i=in_, p=op, a=absv:
                e.tensor_reduce(o, i, AX, p, apply_absolute_value=a))

    def vrecip(self, out, in_):
        self._v(lambda e, o=out, i=in_: e.reciprocal(o, i))

    def vmemset(self, ap, val):
        self._v(lambda e, o=ap, v=val: e.memset(o, v))

    def act(self, out, in_, func, bias=None, scale=1.0, accum=None):
        b = bias if bias is not None else self.zero1[:, 0:1]
        self._s(lambda e, o=out, i=in_, f=func, bb=b, sc=scale, ac=accum:
                e.activation(o, i, f, bias=bb, scale=sc, accum_out=ac))

    def ssqrt(self, out, in_):
        self._s(lambda e, o=out, i=in_: e.sqrt(o, i))

    def build(self):
        self.setup()
        for s in range(NSEQ):
            self.phase1(s)
        self.p15_prologue()
        for s in range(NSEQ):
            self.phase15(s)
        self.barrier()
        self.al.pos = self.base_pos
        self.wdqT = self.al.take(22 * D // 2)
        self.wd_pos = self.al.pos
        self.wgqT = self.al.take(IFF * 8 // 2)
        self.wupqT = self.al.take(IFF * 8 // 2)
        self.ffn_pos = self.al.pos
        self.ffn_weight_prep(self.d_wg, self.wgqT, self.swg)
        self.ffn_weight_prep(self.d_wu, self.wupqT, self.swu)
        self.wd_weight_prep()
        self.collective(0)
        if TRUNC:
            self.barrier()
            self.dma("gpsimd", self.sSP, self.d_dbg1, self.wgqT.bitcast(BF16)[:, 0:16384])
            self.finish()
            return self._materialize()
        for s in range(NSEQ):
            self.phase2(s)
        self.barrier()
        self.collective(1)
        for s in range(NSEQ):
            self.phase3(s)
        self.finish()
        return self._materialize()

    def _materialize(self):
        nc = self.nc
        with nc.Block() as block:
            @block.sync
            def _(e):
                for f in self.prog["sync"]:
                    f(e)
            @block.tensor
            def _(e):
                for f in self.prog["tensor"]:
                    f(e)
            @block.vector
            def _(e):
                for f in self.prog["vector"]:
                    f(e)
            @block.scalar
            def _(e):
                for f in self.prog["scalar"]:
                    f(e)
            @block.gpsimd
            def _(e):
                for f in self.prog["gpsimd"]:
                    f(e)
        return nc

    def setup(self):
        self.dma("sync", self.sX, self.ident[:, :], self.d_ident)
        self.dma("sync", self.sX, self.ones1[:, :], self.d_ones1)
        self.dma("sync", self.sX, self.twden[:, : NW * 4], self.d_twden)
        self.dma("sync", self.sX, self.bias_bc, self.d_outb_bc)
        self.dma("sync", self.sX, self.mixb, self.d_mixb)
        self.dma("sync", self.sX, self.convb, self.d_convb)
        self.dma("sync", self.sX, self.sc1[0:1, 0:2], self.d_clip)
        self.vmemset(self.zero1, 0.0)
        self.vmemset(self.gmax, 0.0)
        self.vmemset(self.p_rnd, RND)

    def phase1(self, s):
        self.mark(f'phase1({s})')
        al = self.al
        al.pos = self.base_pos
        x_tm = al.take(NT * D)
        p2start = al.pos
        P2 = al.take(4096)
        numstart = al.pos
        num = al.take(4 * D)
        xfmstart = al.pos
        xfm_h = al.take(8 * S // 2)
        xfm_l = al.take(8 * S // 2)
        stg_h = al.take(D)
        stg_l = al.take(D)
        gt_tm = al.take(D)
        gt_h = al.take(D // 2)
        gt_l = al.take(D // 2)
        gtones = al.take(NH * 65 + 7)
        srec = al.take(32)
        xfm_hb = xfm_h.bitcast(BF16)
        xfm_lb = xfm_l.bitcast(BF16)
        stg_hb = stg_h.bitcast(BF16)
        stg_lb = stg_l.bitcast(BF16)
        gt_hb = gt_h.bitcast(BF16)
        gt_lb = gt_l.bitcast(BF16)
        pb = self.pb

        if s > 0:
            self.barrier()
        self.dma("sync", self.sX,
                 x_tm[:, 0 : 8 * D].rearrange("p (c f) -> p c f", f=D),
                 self.d_x[s].rearrange("(c p) f -> p c f", p=128)[:, 0:8, :])
        ldx = self.sX.n
        self.dma("gpsimd", self.sSP,
                 x_tm[:, 8 * D : 16 * D].rearrange("p (c f) -> p c f", f=D),
                 self.d_x[s].rearrange("(c p) f -> p c f", p=128)[:, 8:16, :])
        ldx_b = self.sSP.n

        for tc in range(NT):
            b = tc % 2
            xs = x_tm[:, tc * D : (tc + 1) * D]
            sh = stg_hb[:, b * D : (b + 1) * D]
            sl = stg_lb[:, b * D : (b + 1) * D]
            if tc < 8:
                self.wait("vector", self.sX, ldx)
            else:
                self.wait("vector", self.sSP, ldx_b)
            if tc >= 2:
                self.wait("vector", self.sTP)
            self.vcopy(sh, xs)
            self.vtt(sl, xs, sh, OP.subtract)
            self.wait("scalar", self.sV)
            for (src, dstf) in ((sh, xfm_hb), (sl, xfm_lb)):
                self.dmatp(
                    dstf.rearrange("f (c t) -> f c t", t=S)[:, :, tc * 128 : (tc + 1) * 128],
                    src.rearrange("t (c f) -> t c f", f=128))
        tp_x = self.sTP.n

        self.mark('conv')
        x256 = al.take(2048)
        comp = al.take(512)
        cwb = al.slice(p2start, 8192)
        self.wait("tensor", self.sX, ldx)
        for i in range(16):
            tc, fch = i % 2, i // 2
            bank = pb[6 + i % 2]
            if i >= 2:
                self.wait("tensor", self.sV)
            self.tp(bank[:, 0:128], x_tm[:, tc * D + fch * 128 : tc * D + fch * 128 + 128])
            self.wait("vector", self.sT)
            self.vcopy(x256[:, fch * 256 + tc * 128 : fch * 256 + tc * 128 + 128], bank[:, 0:128])
        x256r = x256.rearrange("p (c t k) -> p c k t", k=4, t=64)
        self.wait("tensor", self.sV)
        nmm = 0
        cw_mm = {}
        for ich in range(8):
            cb = ich % 2
            if ich >= 2:
                self.wait("sync", self.sT, cw_mm[cb])
            self.dma("sync", self.sCT[cb], cwb[:, cb * 4096 : (cb + 1) * 4096],
                     self.d_cwT[ich * 128 : (ich + 1) * 128, :])
            self.wait("tensor", self.sCT[cb])
            for k in range(4):
                for och in range(8):
                    self.mm(pb[0][:, och * 64 : och * 64 + 64],
                            cwb[:, cb * 4096 + k * D + och * 128 : cb * 4096 + k * D + och * 128 + 128],
                            x256r[:, ich, k, :],
                            start=(nmm == 0), stop=(nmm == 255), inc=True)
                    nmm += 1
            cw_mm[cb] = self.sT.n
        self.wait("vector", self.sT)
        for och in range(8):
            self.vts(comp[:, och * 64 : och * 64 + 64], pb[0][:, och * 64 : och * 64 + 64],
                     self.convb[:, och : och + 1], None, OP.add)
        self.dma("sync", self.sX, gt_tm[64:128, :], self.d_gm)
        gm_n = self.sX.n
        self.wait("tensor", self.sV)
        for och in range(8):
            bank = pb[6 + och % 2]
            if och >= 2:
                self.wait("tensor", self.sV)
            self.tp(bank[0:64, 0:128], comp[:, och * 64 : och * 64 + 64])
            self.wait("vector", self.sT)
            self.vcopy(gt_tm[0:64, och * 128 : och * 128 + 128], bank[0:64, 0:128])
        self.wait("vector", self.sX, gm_n)
        self.wait("vector", self.sTP)
        self.vcopy(stg_hb[:, 0:D], gt_tm)
        self.vtt(stg_lb[:, 0:D], gt_tm, stg_hb[:, 0:D], OP.subtract)
        self.wait("scalar", self.sV)
        self.dmatp(gt_hb.rearrange("f (c k) -> f c k", k=128),
                   stg_hb[:, 0:D].rearrange("k (c f) -> k c f", f=128))
        self.dmatp(gt_lb.rearrange("f (c k) -> f c k", k=128),
                   stg_lb[:, 0:D].rearrange("k (c f) -> k c f", f=128))
        self.vmemset(gtones[:, : NH * 65], 1.0)
        for h in range(NH):
            self.vcopy(gtones[:, h * 65 : h * 65 + 64], gt_tm[:, h * 64 : h * 64 + 64])
        gt_v = self.sV.n
        gt_tp = self.sTP.n

        self.wait("tensor", self.sSP, ldx_b)
        self.mark('cga')
        glob_ev = num
        self.wait("tensor", self.sTP, tp_x)
        self.wait("tensor", self.sTP, gt_tp)
        self.wait("tensor", self.sV, gt_v)
        guard_l = {}
        guard_p2 = {}
        guard_pv = {}
        pend = []
        for tau in range(4):
            for hp in range(8):
                lmm = {}
                for par in (0, 1):
                    if guard_l.get(par):
                        self.wait("tensor", self.sS, guard_l[par])
                for pidx in range(3):
                    for par in (0, 1):
                        h = 2 * hp + par
                        pl = pb[par]
                        bp = par * 64
                        fc = hp
                        glh = gt_hb[bp : bp + 64, fc * 128 : fc * 128 + 128]
                        gll = gt_lb[bp : bp + 64, fc * 128 : fc * 128 + 128]
                        xrh = xfm_hb[bp : bp + 64, fc * S + tau * 512 : fc * S + tau * 512 + 512]
                        xrl = xfm_lb[bp : bp + 64, fc * S + tau * 512 : fc * S + tau * 512 + 512]
                        ops = ((glh, xrh), (glh, xrl), (gll, xrh))[pidx]
                        self.mm(pl[:, :], ops[0], ops[1], start=(pidx == 0), stop=(pidx == 2),
                                inc=(pidx == 2))
                        lmm[par] = self.sT.n
                for p_ in pend:
                    self._cga_pv(*p_)
                pend = []
                for par in (0, 1):
                    h = 2 * hp + par
                    slot = P2[:, par * 512 : par * 512 + 512]
                    if guard_p2.get(par):
                        self.wait("scalar", self.sT, guard_p2[par])
                    self.wait("scalar", self.sT, lmm[par])
                    self.act(slot, pb[par][:, :], AF.Exp, scale=0.125)
                    guard_l[par] = self.sS.n
                    pend.append((tau, h, par, slot, self.sS.n, guard_pv, guard_p2, glob_ev, gtones))
            for p_ in pend:
                self._cga_pv(*p_)
            pend = []
            for qc in range(4):
                b = qc % 2
                gl = glob_ev[:, qc * D : (qc + 1) * D]
                self.wait("vector", self.sST[b])
                self.wait("vector", self.sTP)
                self.vcopy(stg_hb[:, b * D : (b + 1) * D], gl)
                self.vtt(stg_lb[:, b * D : (b + 1) * D], gl, stg_hb[:, b * D : (b + 1) * D], OP.subtract)
                self.wait("gpsimd", self.sV)
                qg = tau * 4 + qc
                self.dma("gpsimd", self.sST[b], self.d_glo_h[s, qg * 128 : (qg + 1) * 128, :],
                         stg_hb[:, b * D : (b + 1) * D])
                self.dma("gpsimd", self.sST[b], self.d_glo_l[s, qg * 128 : (qg + 1) * 128, :],
                         stg_lb[:, b * D : (b + 1) * D])

        self.wait("tensor", self.sV)
        self.wait("tensor", self.sS)
        guard_l = {}
        guard_p2 = {}
        guard_pv = {}
        pend = []
        for w in range(NW):
            for hp in range(8):
                for j in range(4):
                    lmm = {}
                    for par in (0, 1):
                        jb = (par * 2) + (j % 2)
                        if guard_l.get(jb):
                            self.wait("tensor", self.sS, guard_l[jb])
                    for pidx in range(3):
                        for par in (0, 1):
                            h = 2 * hp + par
                            bp = par * 64
                            fc = hp
                            jb = (par * 2) + (j % 2)
                            pl = pb[jb]
                            kc0 = fc * S + w * 256
                            xah = xfm_hb[bp : bp + 64, kc0 + j * 128 : kc0 + j * 128 + 128]
                            xal = xfm_lb[bp : bp + 64, kc0 + j * 128 : kc0 + j * 128 + 128]
                            xkh = xfm_hb[bp : bp + 64, kc0 : kc0 + 512]
                            xkl = xfm_lb[bp : bp + 64, kc0 : kc0 + 512]
                            ops = ((xah, xkh), (xah, xkl), (xal, xkh))[pidx]
                            self.mm(pl[:, :], ops[0], ops[1], start=(pidx == 0), stop=(pidx == 2),
                                    inc=(pidx == 2))
                            lmm[par] = self.sT.n
                    if j == 0 and pend:
                        for p_ in pend:
                            self._swa_pv(*p_)
                        pend = []
                    for par in (0, 1):
                        jb = (par * 2) + (j % 2)
                        slot = P2[:, (par * 4 + j) * 512 : (par * 4 + j) * 512 + 512]
                        if guard_p2.get(par):
                            self.wait("scalar", self.sT, guard_p2[par])
                        if j == 0:
                            self.wait("scalar", self.sV)
                        self.wait("scalar", self.sT, lmm[par])
                        self.act(slot, pb[jb][:, :], AF.Exp, scale=0.125,
                                 accum=srec[:, par * 4 + j : par * 4 + j + 1])
                        guard_l[jb] = self.sS.n
                expall = self.sS.n
                for par in (0, 1):
                    h = 2 * hp + par
                    rs = srec[:, 16 + par * 4 : 16 + par * 4 + 4]
                    self.wait("vector", self.sS, expall)
                    self.vrecip(rs, srec[:, par * 4 : par * 4 + 4])
                    self.vtt(rs, rs, self.twden[:, w * 4 : w * 4 + 4], OP.mult)
                    pend.append((s, w, h, par, rs, self.sV.n, expall, P2, x_tm, num, guard_pv, guard_p2))
            for p_ in pend:
                self._swa_pv(*p_)
            pend = []
            final = [2 * w, 2 * w + 1] + ([14, 15] if w == NW - 1 else [])
            for qg in final:
                slot = qg % 4
                b = slot % 2
                nsl = num[:, slot * D : (slot + 1) * D]
                self.wait("vector", self.sST[b])
                self.wait("vector", self.sTP)
                self.vcopy(stg_hb[:, b * D : (b + 1) * D], nsl)
                self.vtt(stg_lb[:, b * D : (b + 1) * D], nsl, stg_hb[:, b * D : (b + 1) * D], OP.subtract)
                self.wait("gpsimd", self.sV)
                self.dma("gpsimd", self.sST[b], self.d_loc_h[s, qg * 128 : (qg + 1) * 128, :],
                         stg_hb[:, b * D : (b + 1) * D])
                self.dma("gpsimd", self.sST[b], self.d_loc_l[s, qg * 128 : (qg + 1) * 128, :],
                         stg_lb[:, b * D : (b + 1) * D])

        ctb = al.slice(xfmstart, 16384)
        bcev = num
        self.wait("vector", self.sST[0])
        self.wait("vector", self.sST[1])
        self.barrier(("sync", "tensor", "vector"))
        ev_done = 0
        for t5 in range(4):
            b = t5 % 2
            if t5 >= 2:
                self.wait("sync", self.sT)
            self.dma("sync", self.sCT[b], ctb[:, b * 8192 : (b + 1) * 8192],
                     self.d_CT[t5])
            self.wait("tensor", self.sCT[b])
            for dch in range(8):
                bank = pb[4 + dch % 2]
                if dch >= 2 or t5 > 0:
                    self.wait("tensor", self.sV, ev_done)
                for sch in range(16):
                    self.mm(bank[:, :],
                            x_tm[:, sch * D + dch * 128 : sch * D + dch * 128 + 128],
                            ctb[:, b * 8192 + sch * 512 : b * 8192 + sch * 512 + 512],
                            start=(sch == 0), stop=(sch == 15), inc=(sch == 15))
                mm_n = self.sT.n
                self.wait("vector", self.sT, mm_n)
                if dch == 0:
                    self.wait("vector", self.sSB)
                self.vcopy(bcev[:, dch * 512 : dch * 512 + 512], bank[:, :])
                ev_done = self.sV.n
            self.wait("gpsimd", self.sV)
            self.dma("gpsimd", self.sSB,
                     self.d_bc[s, t5].rearrange("c p t -> p c t"),
                     bcev[:, 0:4096].rearrange("p (c t) -> p c t", t=512))

    def _cga_pv(self, tau, h, par, slot, expn, guard_pv, guard_p2, glob_ev, gtones):
        pv = self.pb[4 + par]
        if guard_pv.get(par):
            self.wait("tensor", self.sV, guard_pv[par])
        self.wait("tensor", self.sS, expn)
        for qc in range(4):
            self.mm(pv[:, qc * 65 : qc * 65 + 65],
                    slot[:, qc * 128 : qc * 128 + 128],
                    gtones[:, h * 65 : h * 65 + 65],
                    start=(qc == 0), stop=(qc == 3), inc=(qc == 3))
        guard_p2[par] = self.sT.n
        self.wait("vector", self.sT, self.sT.n)
        for qc in range(4):
            tmp = self.sc1[:, 10 + par : 11 + par]
            self.vrecip(tmp, pv[:, qc * 65 + 64 : qc * 65 + 65])
            self.vts(glob_ev[:, qc * D + h * 64 : qc * D + h * 64 + 64],
                     pv[:, qc * 65 : qc * 65 + 64], tmp, None, OP.mult)
        guard_pv[par] = self.sV.n

    def _swa_pv(self, s, w, h, par, rs, rn, expall, P2, x_tm, num, guard_pv, guard_p2):
        pv = self.pb[4 + par]
        if guard_pv.get(par):
            self.wait("tensor", self.sV, guard_pv[par])
        self.wait("tensor", self.sS, expall)
        for j in range(4):
            for kc in range(4):
                tv = 2 * w + kc
                self.mm(pv[:, j * 64 : j * 64 + 64],
                        P2[:, (par * 4 + kc) * 512 + j * 128 : (par * 4 + kc) * 512 + j * 128 + 128],
                        x_tm[:, tv * D + h * 64 : tv * D + h * 64 + 64],
                        start=(j == 0 and kc == 0), stop=(j == 3 and kc == 3),
                        inc=(j == 3 and kc == 3))
        guard_p2[par] = self.sT.n
        pvn = self.sT.n
        self.wait("vector", self.sT, pvn)
        self.wait("vector", self.sV, rn)
        for j in range(4):
            qg = 2 * w + j
            slot = qg % 4
            init = (w == 0) or (j >= 2)
            dst = num[:, slot * D + h * 64 : slot * D + h * 64 + 64]
            src = pv[:, j * 64 : j * 64 + 64]
            if init:
                self.vts(dst, src, rs[:, j : j + 1], None, OP.mult)
            else:
                self.vstt(dst, src, rs[:, j : j + 1], dst, OP.mult, OP.add)
        guard_pv[par] = self.sV.n

    def p15_prologue(self):
        al = self.al
        self.barrier()
        al.pos = self.base_pos
        self.mixw_h = al.take(8192)
        self.mixw_l = al.take(8192)
        self.outw_h = al.take(4096)
        self.outw_l = al.take(4096)
        self.p15_pos = al.pos
        stg = al.take(2048)
        mwh = self.mixw_h.bitcast(BF16)
        mwl = self.mixw_l.bitcast(BF16)
        owh = self.outw_h.bitcast(BF16)
        owl = self.outw_l.bitcast(BF16)
        chunks = [(self.d_mixw, mwh, mwl, c) for c in range(16)] + \
                 [(self.d_outw, owh, owl, c) for c in range(8)]
        ld = {}
        g = {}

        def pro_load(k):
            dram, _, _, c = chunks[k]
            b = k % 2
            eng, sem = (("sync", self.sW), ("gpsimd", self.sSP))[b]
            if g.get(b):
                self.wait(eng, self.sV, g[b])
            self.dma(eng, sem, stg[:, b * D : (b + 1) * D], dram[c * 128 : (c + 1) * 128, :])
            ld[k] = sem.n

        pro_load(0)
        for k, (dram, hh, ll, c) in enumerate(chunks):
            b = k % 2
            if k + 1 < len(chunks):
                pro_load(k + 1)
            self.wait("vector", (self.sW, self.sSP)[b], ld[k])
            ss = stg[:, b * D : (b + 1) * D]
            self.vcopy(hh[:, c * D : (c + 1) * D], ss)
            self.vtt(ll[:, c * D : (c + 1) * D], ss, hh[:, c * D : (c + 1) * D], OP.subtract)
            g[b] = self.sV.n
        al.pos = self.p15_pos

    def phase15(self, s):
        self.mark(f'phase15({s})')
        al = self.al
        al.pos = self.p15_pos
        lgf_lh = al.take(2048)
        lgf_ll = al.take(2048)
        lgf_gh = al.take(2048)
        lgf_gl = al.take(2048)
        bcf = al.take(4096)
        xres = al.take(4096)
        gate = al.take(1024)
        mxh = al.take(2048)
        mxl = al.take(2048)
        y1 = al.take(1024)
        lns = al.take(1024)
        lnv = al.take(16)
        x1t = al.take(1024)
        lhb = lgf_lh.bitcast(BF16)
        llb = lgf_ll.bitcast(BF16)
        ghb = lgf_gh.bitcast(BF16)
        glb = lgf_gl.bitcast(BF16)
        mxhb = mxh.bitcast(BF16)
        mxlb = mxl.bitcast(BF16)
        mwh = self.mixw_h.bitcast(BF16)
        mwl = self.mixw_l.bitcast(BF16)
        owh = self.outw_h.bitcast(BF16)
        owl = self.outw_l.bitcast(BF16)
        pb = self.pb
        self.barrier()

        def p15_loads(tau, mm_guard, chain_guard):
            t0 = tau * 512
            if mm_guard:
                self.wait("scalar", self.sT, mm_guard)
                self.wait("scalar", self.sV, chain_guard)
            for (dram, dst) in ((self.d_loc_h[s], lhb), (self.d_loc_l[s], llb),
                                (self.d_glo_h[s], ghb), (self.d_glo_l[s], glb)):
                self.dmatp(dst.rearrange("f (c t) -> f c t", t=512),
                           dram[t0 : t0 + 512, :].rearrange("t (c f) -> t c f", f=128))
            if chain_guard:
                self.wait("sync", self.sV, chain_guard)
            self.dma("sync", self.sBCF, bcf.rearrange("p (c t) -> p c t", t=512),
                     self.d_bc[s, tau].rearrange("c p t -> p c t"))
            return (self.sBCF.n, self.sTP.n)

        def p15_xres(tau, part, guard):
            if guard:
                self.wait("sync", self.sV, guard)
            self.dma("sync", self.sXR[part],
                     xres[:, part * 2048 : (part + 1) * 2048].rearrange("p (c f) -> p c f", f=D),
                     self.d_x[s].rearrange("(c p) f -> p c f", p=128)[:, 4 * tau + 2 * part : 4 * tau + 2 * part + 2, :])
            return self.sXR[part].n

        next_ld = p15_loads(0, 0, 0)
        xres_ld = {0: p15_xres(0, 0, 0), 1: p15_xres(0, 1, 0)}
        for tau in range(4):
            ld_n, tp_n = next_ld
            self.wait("tensor", self.sBCF, ld_n)
            self.wait("tensor", self.sTP, tp_n)
            sig_done = {}
            chain_done = {}
            for och in range(8):
                pg = pb[och % 2]
                if sig_done.get(och % 2):
                    self.wait("tensor", self.sS, sig_done[och % 2])
                for hch in range(16):
                    if hch < 8:
                        ah = lhb[:, hch * 512 : hch * 512 + 512]
                        alo = llb[:, hch * 512 : hch * 512 + 512]
                    else:
                        ah = ghb[:, (hch - 8) * 512 : (hch - 8) * 512 + 512]
                        alo = glb[:, (hch - 8) * 512 : (hch - 8) * 512 + 512]
                    wh = mwh[:, hch * D + och * 128 : hch * D + och * 128 + 128]
                    wl = mwl[:, hch * D + och * 128 : hch * D + och * 128 + 128]
                    self.mm(pg[:, :], wh, ah, hch == 0, False)
                    self.mm(pg[:, :], wh, alo, False, False)
                    self.mm(pg[:, :], wl, ah, False, hch == 15, inc=(hch == 15))
                g_n = self.sT.n
                gslot = gate[:, (och % 2) * 512 : (och % 2) * 512 + 512]
                self.wait("scalar", self.sT, g_n)
                if chain_done.get(och % 2):
                    self.wait("scalar", self.sV, chain_done[och % 2])
                self.act(gslot, pg[:, :], AF.Sigmoid, bias=self.mixb[:, och : och + 1])
                sig_done[och % 2] = self.sS.n
                sg_n = self.sS.n
                lh_ = lhb[:, och * 512 : och * 512 + 512]
                ll_ = llb[:, och * 512 : och * 512 + 512]
                gh_ = ghb[:, och * 512 : och * 512 + 512]
                gl_ = glb[:, och * 512 : och * 512 + 512]
                t1 = lns[:, 0:512]
                t2 = lns[:, 512:1024]
                self.wait("vector", self.sS, sg_n)
                self.vtt(t1, lh_, ll_, OP.add)
                self.vtt(t2, gh_, gl_, OP.add)
                self.vtt(t1, t1, t2, OP.subtract)
                self.vtt(gslot, gslot, t1, OP.mult)
                self.vtt(gslot, gslot, t2, OP.add)
                self.vtt(gslot, gslot, bcf[:, och * 512 : och * 512 + 512], OP.add)
                self.vcopy(mxhb[:, och * 512 : och * 512 + 512], gslot)
                self.vtt(mxlb[:, och * 512 : och * 512 + 512], gslot,
                         mxhb[:, och * 512 : och * 512 + 512], OP.subtract)
                chain_done[och % 2] = self.sV.n
            mx_n = self.sV.n
            if tau < 3:
                next_ld = p15_loads(tau + 1, self.sT.n, mx_n)
            self.wait("tensor", self.sV, mx_n)
            for ts in range(4):
                tg = 4 * tau + ts
                po0, po1 = pb[2], pb[3]
                if getattr(self, "_y1ev", 0):
                    self.wait("tensor", self.sV, self._y1ev)
                for hch in range(8):
                    mh = mxhb[:, hch * 512 + ts * 128 : hch * 512 + ts * 128 + 128]
                    mlo = mxlb[:, hch * 512 + ts * 128 : hch * 512 + ts * 128 + 128]
                    for (pbk, ocol) in ((po0, 0), (po1, 512)):
                        wh = owh[:, hch * D + ocol : hch * D + ocol + 512]
                        wl = owl[:, hch * D + ocol : hch * D + ocol + 512]
                        self.mm(pbk[:, :], mh, wh, hch == 0, False)
                        self.mm(pbk[:, :], mlo, wh, False, False)
                        self.mm(pbk[:, :], mh, wl, False, hch == 7,
                                inc=(hch == 7 and ocol == 512))
                o_n = self.sT.n
                self.wait("vector", self.sT, o_n)
                if getattr(self, "_ln1s", 0):
                    self.wait("vector", self.sS, self._ln1s)
                for (pbk, ocol) in ((po0, 0), (po1, 512)):
                    self.vtt(y1[:, ocol : ocol + 512], pbk[:, :],
                             self.bias_bc[:, ocol : ocol + 512], OP.add)
                self.wait("vector", self.sXR[ts // 2], xres_ld[ts // 2])
                self.vtt(y1, y1, xres[:, ts * D : (ts + 1) * D], OP.add)
                y1_n = self.sV.n
                self._y1ev = y1_n
                if tau < 3 and ts == 1:
                    xres_ld[0] = p15_xres(tau + 1, 0, y1_n)
                if tau < 3 and ts == 3:
                    xres_ld[1] = p15_xres(tau + 1, 1, y1_n)
                self.wait("scalar", self.sV, y1_n)
                self.act(lns, y1, AF.Square, accum=lnv[:, 0:1])
                sq_n = self.sS.n
                self.vred(lnv[:, 1:2], y1, OP.add)
                self.wait("vector", self.sS, sq_n)
                self.wait("scalar", self.sX1)
                self._ln_tail(y1, lnv, x1t)
                self._ln1s = self.sS.n
                self.wait("vector", self.sS)
                self.vred(lnv[:, 8:9], x1t, OP.max, absv=True)
                self.vtt(self.gmax[:, 0:1], self.gmax[:, 0:1], lnv[:, 8:9], OP.max)
                self.wait("gpsimd", self.sS)
                self.dma("gpsimd", self.sX1, self.d_x1[s, tg], x1t)

    def _ln_tail(self, y, lnv, xout):
        nm = lnv[:, 2:3]
        var = lnv[:, 3:4]
        r = lnv[:, 4:5]
        t1 = lnv[:, 5:6]
        t2 = lnv[:, 6:7]
        self.vts(nm, lnv[:, 1:2], -1.0 / D, None, OP.mult)
        self.vtt(t1, nm, nm, OP.mult)
        self.vstt(var, lnv[:, 0:1], 1.0 / D, t1, OP.mult, OP.subtract)
        self.vts(var, var, LNEPS, None, OP.add)
        self.wait("scalar", self.sV)
        self.ssqrt(t1, var)
        self.wait("vector", self.sS)
        self.vrecip(r, t1)
        for _ in range(2):
            self.vtt(t1, r, r, OP.mult)
            self.vtt(t1, t1, var, OP.mult)
            self.vts(t1, t1, -0.5, 1.5, OP.mult, OP.add)
            self.vtt(r, r, t1, OP.mult)
        self.vtt(t2, nm, r, OP.mult)
        self.wait("scalar", self.sV)
        self.act(xout, y, AF.Identity, bias=t2, scale=r)

    def ffn_weight_prep(self, dram, dstT32, scales):
        al = self.al
        al.pos = self.ffn_pos
        dstT = dstT32.bitcast(BF16)
        stg = al.take(2048)
        qf16 = al.take(512)
        qb = al.take(1024)
        f16v = qf16.bitcast(F16)
        qbb = qb.bitcast(BF16)
        amax = al.take(2)
        rw = al.take(2)
        q_done = {}
        self.wait("sync", self.sV)
        self.wait("sync", self.sTP)
        for c in range(22):
            b = c % 2
            st = stg[:, b * D : (b + 1) * D]
            qs = qbb[:, b * D : (b + 1) * D]
            if q_done.get(b):
                self.wait("sync", self.sV, q_done[b])
            self.dma("sync", self.sCT[b], st, dram[c * 128 : (c + 1) * 128, :])
            self.wait("vector", self.sCT[b])
            self.vred(amax[:, 0:1], st, OP.max, absv=True)
            self.vts(scales[:, c : c + 1], amax[:, 0:1], 1.0 / 127.0, None, OP.mult)
            self.vrecip(rw[:, 0:1], amax[:, 0:1])
            self.vts(rw[:, 1:2], rw[:, 0:1], 127.0, None, OP.mult)
            self.wait("scalar", self.sV)
            self.act(f16v[:, 0:D], st, AF.Identity, bias=self.p_rnd[:, 0:1], scale=rw[:, 1:2])
            self.wait("vector", self.sS)
            self.wait("vector", self.sTP)
            self.vts(qs, f16v[:, 0:D], RND, None, OP.subtract)
            q_done[b] = self.sV.n
            self.wait("scalar", self.sV)
            self.dmatp(dstT.rearrange("h (c i) -> h c i", i=IFF)[:, :, c * 128 : (c + 1) * 128],
                       qs.rearrange("i (c h) -> i c h", h=128))

    def wd_weight_prep(self):
        al = self.al
        al.pos = self.ffn_pos
        dstT = self.wdqT.bitcast(BF16)
        stg = al.take(2 * IFF)
        qf16 = al.take(IFF // 2)
        qb = al.take(IFF)
        f16v = qf16.bitcast(F16)
        qbb = qb.bitcast(BF16)
        amax = al.take(2)
        rw = al.take(2)
        q_done = {}
        self.wait("sync", self.sV)
        self.wait("sync", self.sTP)
        for c in range(8):
            b = c % 2
            st = stg[:, b * IFF : (b + 1) * IFF]
            qs = qbb[:, b * IFF : (b + 1) * IFF]
            if q_done.get(b):
                self.wait("sync", self.sV, q_done[b])
            self.dma("sync", self.sCT[b], st, self.d_wd[c * 128 : (c + 1) * 128, :])
            self.wait("vector", self.sCT[b])
            self.vred(amax[:, 0:1], st, OP.max, absv=True)
            self.vts(self.swd[:, c : c + 1], amax[:, 0:1], 1.0 / 127.0, None, OP.mult)
            self.vrecip(rw[:, 0:1], amax[:, 0:1])
            self.vts(rw[:, 1:2], rw[:, 0:1], 127.0, None, OP.mult)
            self.wait("scalar", self.sV)
            self.act(f16v[:, 0:IFF], st, AF.Identity, bias=self.p_rnd[:, 0:1], scale=rw[:, 1:2])
            self.wait("vector", self.sS)
            self.wait("vector", self.sTP)
            self.vts(qs, f16v[:, 0:IFF], RND, None, OP.subtract)
            q_done[b] = self.sV.n
            self.wait("scalar", self.sV)
            self.dmatp(dstT.rearrange("i (c o) -> i c o", o=D)[:, :, c * 128 : (c + 1) * 128],
                       qs.rearrange("o (c i) -> o c i", i=128))

    def collective(self, idx):
        self.barrier()
        pb6 = self.pb[6]
        self.tp(pb6[0:1, 0:128], self.gmax[:, idx : idx + 1])
        self.wait("vector", self.sT)
        self.vred(self.sc1[0:1, 4:5], pb6[0:1, 0:128], OP.max)
        self.wait("gpsimd", self.sV)
        self.dma("gpsimd", self.sCCD, self.d_cc_in.ap(), self.sc1[0:1, 4:5])
        self.wait("gpsimd", self.sCCD)
        self._raw("gpsimd", lambda e: e.collective_compute(
            "AllReduce", OP.max, replica_groups=[list(range(8))],
            ins=[self.d_cc_in.ap()], outs=[self.d_cc_out.ap()]), inc=self.sG)
        self.wait("gpsimd", self.sG)
        self.dma("gpsimd", self.sCCD, self.sc1[0:1, 5:6], self.d_cc_out.ap())
        self.wait("scalar", self.sCCD)
        self.act(self.sc1[0:1, 6:7], self.sc1[0:1, idx : idx + 1], AF.Sigmoid, bias=self.zero1[0:1, 0:1])
        self.wait("vector", self.sS)
        self.vtt(self.sc1[0:1, 7:8], self.sc1[0:1, 5:6], self.sc1[0:1, 6:7], OP.mult)
        self.vts(self.sc1[0:1, 8:9], self.sc1[0:1, 7:8], 1.0 / 127.0, None, OP.mult)
        self.vrecip(self.sc1[0:1, 9:10], self.sc1[0:1, 8:9])
        self.wait("tensor", self.sV)
        self.mm(pb6[:, 0:2], self.ones1[:, :], self.sc1[0:1, 8:10], True, True, inc=True)
        self.wait("vector", self.sT)
        self.vcopy(self.p_sc[:, 2 * idx : 2 * idx + 2], pb6[:, 0:2])
        if idx == 0:
            self.vts(self.swgx[:, 0:22], self.swg[:, 0:22], self.p_sc[:, 0:1], None, OP.mult)
            self.vts(self.swux[:, 0:22], self.swu[:, 0:22], self.p_sc[:, 0:1], None, OP.mult)
        else:
            self.build_swd_bc()
        self.barrier()

    def build_swd_bc(self):
        pb7 = self.pb[7]
        self.wait("gpsimd", self.sCCD)
        self.dma("gpsimd", self.sCCD, self.d_swdscr.ap().rearrange("c p -> p c"),
                 self.swd[:, 0:8], slow=True)
        self.wait("sync", self.sCCD)
        self.dma("sync", self.sW, self.swd_bc[0:1, 0:D],
                 self.d_swdscr.ap().rearrange("c p -> (c p)")[None, :])
        self.wait("tensor", self.sW)
        for k in range(2):
            self.mm(pb7[:, 0:512], self.ones1[:, :], self.swd_bc[0:1, k * 512 : k * 512 + 512],
                    True, True, inc=True)
            self.wait("vector", self.sT)
            self.vts(self.swd_bc[:, k * 512 : k * 512 + 512], pb7[:, 0:512],
                     self.p_sc[:, 2:3], None, OP.mult)
            self.wait("tensor", self.sV)

    def phase2(self, s):
        self.mark(f'phase2({s})')
        al = self.al
        al.pos = self.ffn_pos
        x1q = al.take(8192)
        x1r = al.take(4096)
        qf16 = al.take(512)
        qtm = al.take(512)
        gbuf = al.take(512)
        ubuf = al.take(512)
        hbuf = al.take(1024)
        hmx = al.take(2)
        x1qb = x1q.bitcast(BF16)
        f16v = qf16.bitcast(F16)
        qtmb = qtm.bitcast(BF16)
        wg = self.wgqT.bitcast(BF16)
        wu = self.wupqT.bitcast(BF16)
        pb = self.pb
        self.barrier()
        for tau in range(4):
            self.wait("sync", self.sV)
            self.dma("sync", self.sW, x1r.rearrange("p (c f) -> p c f", f=D),
                     self.d_x1[s, 4 * tau : 4 * tau + 4].rearrange("c p f -> p c f"))
            ld_n = self.sW.n
            for tc in range(4):
                xt = x1r[:, tc * D : (tc + 1) * D]
                self.wait("vector", self.sW, ld_n)
                self.vts(f16v[:, 0:D], xt, self.p_sc[:, 1:2], RND, OP.mult, OP.add)
                self.wait("vector", self.sTP)
                self.vts(qtmb[:, 0:D], f16v[:, 0:D], RND, None, OP.subtract)
                self.vts(qtmb[:, 0:D], qtmb[:, 0:D], 127.0, -128.0, OP.min, OP.max)
                self.wait("scalar", self.sV)
                gtc = 4 * tau + tc
                self.dmatp(x1qb.rearrange("f (c t) -> f c t", t=S)[:, :, gtc * 128 : gtc * 128 + 128],
                           qtmb[:, 0:D].rearrange("t (c f) -> t c f", f=128))
        self.wait("tensor", self.sTP)
        if s == 0 and DEBUG:
            self.wait("gpsimd", self.sTP)
            self.dma("gpsimd", self.sSP, self.d_dbg1, x1qb[:, 0:16384])
        ev_guard = {}
        hspill = {}
        for tau in range(4):
            for ic in range(22):
                pg = pb[0 + 2 * (ic % 2)]
                pu = pb[1 + 2 * (ic % 2)]
                gk = ev_guard.get(ic % 2)
                if gk:
                    self.wait("tensor", self.sV, gk[0])
                    self.wait("tensor", self.sS, gk[1])
                for hch in range(8):
                    self.mm(pg[:, :], wg[:, hch * IFF + ic * 128 : hch * IFF + ic * 128 + 128],
                            x1qb[:, hch * S + tau * 512 : hch * S + tau * 512 + 512],
                            hch == 0, hch == 7, inc=(hch == 7))
                for hch in range(8):
                    self.mm(pu[:, :], wu[:, hch * IFF + ic * 128 : hch * IFF + ic * 128 + 128],
                            x1qb[:, hch * S + tau * 512 : hch * S + tau * 512 + 512],
                            hch == 0, hch == 7, inc=(hch == 7))
                mm_n = self.sT.n
                hslot = hbuf[:, (ic % 2) * 512 : (ic % 2) * 512 + 512]
                self.wait("scalar", self.sT, mm_n)
                self.wait("scalar", self.sV)
                self.act(gbuf, pg[:, :], AF.Silu, scale=self.swgx[:, ic : ic + 1])
                self.act(ubuf, pu[:, :], AF.Identity, bias=self.zero1[:, 0:1],
                         scale=self.swux[:, ic : ic + 1])
                sg_n = self.sS.n
                self.wait("vector", self.sS, sg_n)
                if hspill.get(ic % 2):
                    self.wait("vector", (self.sH, self.sH1)[ic % 2], hspill[ic % 2])
                self.vtt(hslot, gbuf, ubuf, OP.mult)
                self.vred(hmx[:, 0:1], hslot, OP.max, absv=True)
                self.vtt(self.gmax[:, 1:2], self.gmax[:, 1:2], hmx[:, 0:1], OP.max)
                self.wait("gpsimd", self.sV)
                hsem = (self.sH, self.sH1)[ic % 2]
                self.dma("gpsimd", hsem, self.d_h[s, ic, :, tau * 512 : (tau + 1) * 512], hslot)
                hspill[ic % 2] = hsem.n
                ev_guard[ic % 2] = (self.sV.n, self.sS.n)

    def phase3(self, s):
        self.mark(f'phase3({s})')
        al = self.al
        al.pos = self.wd_pos
        hq = al.take(11264)
        hrel = al.take(11264)
        qf16 = al.take(2816)
        x1r = al.take(8192)
        y2 = al.take(1024)
        lns = al.take(1024)
        lnv = al.take(16)
        x2t = al.take(2048)
        hqb = hq.bitcast(BF16)
        f16v = qf16.bitcast(F16)
        wd = self.wdqT.bitcast(BF16)
        pb = self.pb
        self.barrier()
        hq_mm = {}
        quant_done = {}
        half_ld = {}
        hload_guard = {}
        def p3_loads(tau, y2guard):
            xslot = (tau % 2) * 4096
            if y2guard:
                self.wait("sync", self.sV, y2guard)
            self.dma("sync", self.sXR[tau % 2], x1r[:, xslot : xslot + 4096].rearrange("p (c f) -> p c f", f=D),
                     self.d_x1[s, 4 * tau : 4 * tau + 4].rearrange("c p f -> p c f"))
            for half in range(2):
                hs = half * 5632
                hcs = half * 11
                if hload_guard.get(half):
                    self.wait("sync", self.sV, hload_guard[half])
                self.dma("sync", self.sCT[half],
                         hrel[:, hs : hs + 5632].rearrange("p (c t) -> p c t", t=512),
                         self.d_h[s][hcs : hcs + 11, :, tau * 512 : (tau + 1) * 512].rearrange("c p t -> p c t"))
            return self.sXR[tau % 2].n

        x1_ld = {0: p3_loads(0, 0)}
        for tau in range(4):
            hb = (tau % 2) * 11264
            xslot = (tau % 2) * 4096
            ld_x1 = x1_ld[tau]
            if tau >= 2:
                self.wait("vector", self.sT, hq_mm[tau % 2])
            for half in range(2):
                hs = half * 5632
                self.wait("vector", self.sCT[half])
                self.vts(f16v[:, 0:5632], hrel[:, hs : hs + 5632], self.p_sc[:, 3:4], RND,
                         OP.mult, OP.add)
                self.vts(hqb[:, hb + hs : hb + hs + 5632], f16v[:, 0:5632], RND, None, OP.subtract)
                self.vts(hqb[:, hb + hs : hb + hs + 5632], hqb[:, hb + hs : hb + hs + 5632],
                         127.0, -128.0, OP.min, OP.max)
                hload_guard[half] = self.sV.n
            hq_n = self.sV.n
            quant_done[tau] = hq_n
            if tau < 3:
                x1_ld[tau + 1] = p3_loads(tau + 1, getattr(self, "_y2ev", 0) if tau >= 1 else 0)
            self.wait("tensor", self.sV, hq_n)
            self.wait("vector", self.sXR[tau % 2], ld_x1)
            for ts in range(4):
                tg = 4 * tau + ts
                po0, po1 = pb[4], pb[5]
                if getattr(self, "_y2ev", 0):
                    self.wait("tensor", self.sV, self._y2ev)
                for ic in range(22):
                    lh = hqb[:, hb + ic * 512 + ts * 128 : hb + ic * 512 + ts * 128 + 128]
                    self.mm(po0[:, :], lh, wd[:, ic * D : ic * D + 512], ic == 0, ic == 21)
                    self.mm(po1[:, :], lh, wd[:, ic * D + 512 : ic * D + 1024], ic == 0, ic == 21,
                            inc=(ic == 21))
                o_n = self.sT.n
                hq_mm[tau % 2] = o_n
                self.wait("vector", self.sT, o_n)
                if getattr(self, "_ln2s", 0):
                    self.wait("vector", self.sS, self._ln2s)
                for (pbk, ocol) in ((po0, 0), (po1, 512)):
                    self.vtt(y2[:, ocol : ocol + 512], pbk[:, :],
                             self.swd_bc[:, ocol : ocol + 512], OP.mult)
                self.vtt(y2, y2, x1r[:, xslot + ts * D : xslot + (ts + 1) * D], OP.add)
                y2_n = self.sV.n
                self._y2ev = y2_n
                self.wait("scalar", self.sV, y2_n)
                self.act(lns, y2, AF.Square, accum=lnv[:, 0:1])
                sq_n = self.sS.n
                self.vred(lnv[:, 1:2], y2, OP.add)
                self.wait("vector", self.sS, sq_n)
                xb = (tg % 2) * D
                self.wait("vector", self.sOUT[tg % 2])
                self.wait("scalar", self.sOUT[tg % 2])
                self._ln_tail(y2, lnv, x2t[:, xb : xb + D])
                self._ln2s = self.sS.n
                self.wait("gpsimd", self.sS)
                self.dma("gpsimd", self.sOUT[tg % 2], self.d_out[s, tg * 128 : (tg + 1) * 128, :],
                         x2t[:, xb : xb + D])

    def finish(self):
        for sem in (self.sSP, self.sST[0], self.sST[1], self.sSB, self.sX1, self.sH, self.sH1,
                    self.sOUT[0], self.sOUT[1], self.sCCD, self.sBCF, self.sXR[0], self.sXR[1]):
            self.wait("gpsimd", sem)


_BUILT = None


def _build():
    global _BUILT
    if _BUILT is None:
        mk = MK()
        nc = mk.build()
        _BUILT = (mk, nc)
        if DEBUG:
            print(f"[mk] built: {mk.nins} ops, arena peak {mk.al.peak} f32 words "
                  f"({mk.al.peak * 4 // 1024} KiB/partition)")
    return _BUILT


def kernel(**inputs):
    CT, twden, ident, ones1 = _host_consts()
    x = np.ascontiguousarray(np.asarray(inputs["L0_attn_hidden_states"], dtype=np.float32))
    gm = np.asarray(inputs["L0_attn_global_memory"], dtype=np.float32)
    mem = np.ascontiguousarray(gm[0].transpose(1, 0, 2).reshape(G, D))
    cw = np.asarray(inputs["L0_attn_conv_w"], dtype=np.float32)
    cwT = np.ascontiguousarray(cw.transpose(1, 2, 0).reshape(D, 4 * D))
    convb = np.ascontiguousarray(np.asarray(inputs["L0_attn_conv_b"], np.float32).reshape(8, 128).T)
    mixb = np.ascontiguousarray(np.asarray(inputs["L0_attn_mix_b"], np.float32).reshape(8, 128).T)
    outb_bc = np.ascontiguousarray(
        np.broadcast_to(np.asarray(inputs["L0_attn_out_b"], np.float32)[None, :], (128, D)))
    clips = np.stack([np.asarray(inputs["L0_ffn_clip_x"], np.float32)[0],
                      np.asarray(inputs["L0_ffn_clip_down"], np.float32)[0]])[None, :]
    mk, nc = _build()
    common = {
        "gm_mem": mem,
        "cwT": cwT,
        "convb_pp": convb,
        "mix_w": np.ascontiguousarray(np.asarray(inputs["L0_attn_mix_w"], np.float32)),
        "mixb_pp": mixb,
        "out_w": np.ascontiguousarray(np.asarray(inputs["L0_attn_out_w"], np.float32)),
        "outb_bc": outb_bc,
        "clips": np.ascontiguousarray(clips),
        "w_gate": np.ascontiguousarray(np.asarray(inputs["L0_ffn_w_gate"], np.float32)),
        "w_up": np.ascontiguousarray(np.asarray(inputs["L0_ffn_w_up"], np.float32)),
        "w_down": np.ascontiguousarray(np.asarray(inputs["L0_ffn_w_down"], np.float32)),
        "CT": CT,
        "twden": twden,
        "ident": ident,
        "ones1": ones1,
    }
    in_maps = [dict(common, x_in=np.ascontiguousarray(x[2 * c : 2 * c + 2])) for c in range(8)]
    res = run_bass_kernel_spmd(nc, in_maps, core_ids=list(range(8)))
    out = np.concatenate([np.asarray(r["out"]) for r in res.results], axis=0)
    if DEBUG:
        kernel.last_results = res.results
    return out
